# version 06
# baseline (speedup 1.0000x reference)
import numpy as np
import ml_dtypes
from contextlib import ExitStack

import concourse.bass as bass
import concourse.mybir as mybir

F32 = mybir.dt.float32
BF16 = mybir.dt.bfloat16
AF = mybir.ActivationFunctionType
ALU = mybir.AluOpType
AX = mybir.AxisListType
bf = ml_dtypes.bfloat16

D = 2048; S = 2048; NH = 16; HD = 128
NE = 16; NSLOT = 128; ES = NE * NSLOT; HFF = 512
WINDOW = 1024
NT = S // 128; NK = D // 128
LN_EPS = 1e-5; RMS_EPS = 1e-6; L2_EPS = 1e-12
SHIFT = 12.0

TWO_PI = 2.0 * np.pi
CW1 = float(np.float32(np.ldexp(np.round(np.ldexp(TWO_PI, 10)), -10)))
CW2 = float(np.float32(np.ldexp(np.round(np.ldexp(TWO_PI - CW1, 22)), -22)))
CW3 = float(np.float32(TWO_PI - CW1 - CW2))
SC_HI = float(np.float32(1.0 / np.sqrt(np.float64(HD))))
SC_LO = float(np.float32(1.0 / np.sqrt(np.float64(HD)) - np.float64(SC_HI)))


def host_consts(n_masks_pad):
    c = {}
    e0 = np.zeros((128, 128), dtype=bf); e0[0, :] = bf(1.0)
    c["e0"] = e0
    c["onescol"] = np.ones((128, 1), dtype=bf)
    c["onesmat"] = np.ones((128, 128), dtype=bf)
    e0col = np.zeros((128, 1), dtype=np.float32); e0col[0, 0] = 1.0
    c["e0col_f32"] = e0col
    e0f = np.zeros((128, 128), dtype=np.float32); e0f[0, :] = 1.0
    c["e0f"] = e0f
    inv_freq64 = 1.0 / (10000.0 ** (np.arange(0, HD, 2, dtype=np.float64) / HD))
    hi = inv_freq64.astype(np.float32)
    lo = (inv_freq64 - hi.astype(np.float64)).astype(np.float32)
    sin_lhsT = np.zeros((128, 64), dtype=np.float32)
    sin_lhsT[0, :] = hi; sin_lhsT[1, :] = lo
    cos_lhsT = sin_lhsT.copy()
    pih = np.float32(np.pi / 2.0); pil = np.float32(np.float64(np.pi / 2.0) - np.float64(pih))
    cos_lhsT[2, :] = pih; cos_lhsT[3, :] = pil
    c["sin_lhst"] = sin_lhsT; c["cos_lhst"] = cos_lhsT
    pos = np.arange(S, dtype=np.float64)
    pospad = np.zeros((128, S), dtype=np.float32)
    pospad[0, :] = pos; pospad[1, :] = pos; pospad[2, :] = 1.0; pospad[3, :] = 1.0
    c["pospad"] = pospad
    ang_sin = inv_freq64[:, None] * pos[None, :]
    ang_cos = ang_sin + np.pi / 2.0
    c["ksin"] = np.round(ang_sin / TWO_PI).astype(np.float32)
    c["kcos"] = np.round(ang_cos / TWO_PI).astype(np.float32)
    for ang, k in ((ang_sin, c["ksin"]), (ang_cos, c["kcos"])):
        r = ang - k.astype(np.float64) * TWO_PI
        assert np.abs(np.abs(r) - np.pi).min() > 1e-6
    masks, sched, valid = mask_schedule()
    mpad = np.ones((n_masks_pad, 128, 512), dtype=bf)
    mpad[: masks.shape[0]] = masks
    c["masks"] = mpad
    return c, sched, valid


def host_weights(inp):
    w = {}
    Wqkv = np.asarray(inp["L0_attn_Wqkv"])
    def fb_layout(cols):
        return Wqkv[:, cols].reshape(NK, 128, 128).transpose(1, 0, 2).copy()
    blocks = []
    for base in (0, D):
        for pair in range(NH // 2):
            h0, h1 = 2 * pair, 2 * pair + 1
            cA = np.concatenate([base + h0 * HD + np.arange(64), base + h1 * HD + np.arange(64)])
            cB = cA + 64
            blocks.append(fb_layout(cA)); blocks.append(fb_layout(cB))
    w["wqk"] = np.ascontiguousarray(np.stack(blocks))
    w["wv"] = np.ascontiguousarray(Wqkv[:, 2 * D:].reshape(NK, 128, D))
    w["wo"] = np.ascontiguousarray(np.asarray(inp["L0_attn_Wo"]).reshape(NK, 128, D))
    w["mu"] = np.ascontiguousarray(np.asarray(inp["L0_ffn_mu"]).reshape(D, ES).reshape(NK, 128, ES))
    w["w1"] = np.ascontiguousarray(np.asarray(inp["L0_ffn_W1"]).reshape(NE, NK, 128, HFF))
    w["w2"] = np.ascontiguousarray(np.asarray(inp["L0_ffn_W2"]).reshape(NE, 4, 128, D))
    w["b1"] = np.ascontiguousarray(inp["L0_ffn_b1"])
    w["b2"] = np.ascontiguousarray(inp["L0_ffn_b2"])
    w["gamma"] = np.ascontiguousarray(inp["L0_ffn_gamma"])
    w["beta"] = np.ascontiguousarray(inp["L0_ffn_beta"])
    wn = np.zeros((128, 32), dtype=np.float32)
    qw = np.asarray(inp["L0_attn_q_norm_w"]); kw = np.asarray(inp["L0_attn_k_norm_w"])
    i = 0
    for vec in (qw, kw):
        for pair in range(NH // 2):
            h0, h1 = 2 * pair, 2 * pair + 1
            cA = np.concatenate([h0 * HD + np.arange(64), h1 * HD + np.arange(64)])
            wn[:, i] = vec[cA]; wn[:, i + 1] = vec[cA + 64]
            i += 2
    w["wnorm_in"] = wn
    w["scale_in"] = np.ascontiguousarray(np.asarray(inp["L0_ffn_scale"]).reshape(1))
    return w


def mask_schedule():
    masks, sched, valid = [], {}, []
    for g in range(4):
        q0 = 512 * g
        for j in range(-8, 4):
            kb = q0 + 128 * j
            if kb + 128 <= 0:
                continue
            valid.append((g, j))
            kq = np.arange(128)[:, None] + kb
            qq = np.arange(512)[None, :] + q0
            m = (kq <= qq) & (qq - kq <= WINDOW - 1)
            if m.all():
                continue
            mm = m.astype(np.float32).astype(bf)
            for i, prev in enumerate(masks):
                if (prev == mm).all():
                    sched[(g, j)] = i
                    break
            else:
                masks.append(mm); sched[(g, j)] = len(masks) - 1
    return np.stack(masks), sched, valid


N_MASKS = 16
G_ORDER = [0, 1, 2, 3]
EMIT_ATTN = True
EMIT_FFN2 = True
EMIT_FFN1 = True
FFN_STAGES = "ABCDE"
SBUF_TAPS = []


class Sem:
    def __init__(self, h):
        self.h = h; self.n = 0
    def inc(self, instr, k=1):
        instr.then_inc(self.h, k); self.n += k
        return instr


def build_program(debug_taps=()):
    _, sched, valid = host_consts(N_MASKS)
    nc = bass.Bass("TRN2", target_bir_lowering=False, detect_race_conditions=False)
    ctx = ExitStack()
    with ctx:
        _emit(nc, ctx, sched, valid, set(debug_taps))
    return nc


def _emit(nc, ctx, mask_sched, mask_valid, debug_taps):
    DI = lambda n, sh, dt=F32: nc.dram_tensor(n, sh, dt, kind="ExternalInput")
    DS = lambda n, sh, dt=BF16: nc.dram_tensor(n, sh, dt)
    x_in = DI("x", [S, D])
    wqk_in = DI("wqk", [32, 128, NK, 128])
    wv_in = DI("wv", [NK, 128, D])
    wo_in = DI("wo", [NK, 128, D])
    mu_in = DI("mu", [NK, 128, ES])
    w1_in = DI("w1", [NE, NK, 128, HFF])
    w2_in = DI("w2", [NE, 4, 128, D])
    b1_in = DI("b1", [NE, HFF])
    b2_in = DI("b2", [NE, D])
    gamma_in = DI("gamma", [D])
    beta_in = DI("beta", [D])
    wnorm_in = DI("wnorm_in", [128, 32])
    scale_in = DI("scale_in", [1])
    e0_in = DI("e0", [128, 128], BF16)
    e0col_in = DI("e0col_f32", [128, 1])
    e0f_in = DI("e0f", [128, 128])
    onescol_in = DI("onescol", [128, 1], BF16)
    onesmat_in = DI("onesmat", [128, 128], BF16)
    sin_l_in = DI("sin_lhst", [128, 64])
    cos_l_in = DI("cos_lhst", [128, 64])
    pospad_in = DI("pospad", [128, S])
    ksin_in = DI("ksin", [64, S])
    kcos_in = DI("kcos", [64, S])
    masks_in = DI("masks", [N_MASKS, 128, 512], BF16)
    out = nc.dram_tensor("out", [S, D], F32, kind="ExternalOutput")

    MUN = DS("MUN", [NK, 128, ES])
    X2 = DS("X2", [S, D], F32)
    X3 = DS("X3", [S, D], F32)
    XLN = DS("XLN", [S, D])
    XN = DS("XN", [S, D])
    EXPL = DS("EXPL", [S, ES])
    SIT = DS("SIT", [NK, 128, ES])
    SOT = DS("SOT", [NE, 128, D])
    IRQ = DS("IRQ", [1, S], F32)
    HSCR = DS("HSCR", [128, HFF])
    XLN2 = DS("XLN2", [S, D])
    QT = DS("QT", [D, S])
    KT = DS("KT", [D, S])
    VT = DS("VT", [S, D])
    dbg = {}
    for name in debug_taps:
        pass
    tap_out = {}
    for name in debug_taps:
        src = {"XLN": XLN, "XN": XN, "EXPL": EXPL, "SIT": SIT, "X2": X2, "X3": X3,
               "XLN2": XLN2, "QT": QT, "KT": KT, "VT": VT, "MUN": MUN, "SOT": SOT}[name]
        shp = list(src.shape)
        tap_out[name] = (src, nc.dram_tensor("tap_" + name, shp,
                                             src.dtype if hasattr(src, "dtype") else BF16,
                                             kind="ExternalOutput"))

    SB = lambda n, sh, dt=F32: ctx.enter_context(nc.sbuf_tensor(n, sh, dt))
    e0 = SB("c_e0", [128, 128], BF16)
    e0col = SB("c_e0col", [128, 1], F32)
    e0f = SB("c_e0f", [128, 128], F32)
    onescol = SB("c_ones", [128, 1], BF16)
    onesmat = SB("c_onesm", [128, 128], BF16)
    scalepad = SB("scalepad", [128, 1], F32)
    scale_bc = SB("scale_bc", [128, 1], F32)
    gam_bc = SB("gam_bc", [128, D], BF16)
    bet_bc = SB("bet_bc", [128, D], BF16)
    wnorm = SB("wnorm", [128, 32], F32)
    cos2 = SB("cos2", [128, S], BF16)
    sin2 = SB("sin2", [128, S], BF16)
    sinl = SB("sinl", [128, 64], F32)
    cosl = SB("cosl", [128, 64], F32)
    row_a_t = SB("row_a_t", [1, ES], F32)
    row_b_t = SB("row_b_t", [1, ES], F32)
    zpad = SB("zpad", [128, ES], BF16)
    zpad_inv = SB("zpad_inv", [128, ES], F32)
    invzc_cols = SB("invzc_cols", [128, NE], F32)
    zrow_ih = SB("zrow_ih", [128, NT], F32)
    b1pad = SB("b1pad", [128, HFF], BF16)
    b2pad = SB("b2pad", [128, D], BF16)
    invrk_cols = SB("invrk_cols", [128, NT], F32)
    invrq_g = SB("invrq_g", [128, 512], F32)
    invz_bc = SB("invz_bc", [128, 512], F32)
    st_m = SB("st_m", [128, 1], F32)
    st_s = SB("st_s", [128, 1], F32)
    st_t = SB("st_t", [128, 1], F32)
    st_u = SB("st_u", [128, 1], F32)
    shiftcol = SB("shiftcol", [128, 1], F32)
    zr4 = SB("zr4", [128, 4], F32)
    qsq = SB("qsq", [128, 512], BF16)
    xt = SB("xt", [128, D], F32)
    xt2 = SB("xt2", [128, D], F32)
    lnt = SB("lnt", [128, D], BF16)
    xnt = SB("xnt", [128, D], BF16)
    big0 = SB("big0", [128, 32768], BF16)
    wbuf = SB("wbuf", [128, 4096], BF16)
    w2buf0 = SB("w2buf0", [128, D], BF16)
    w2buf1 = SB("w2buf1", [128, D], BF16)
    xg = SB("xg", [128, 8192], BF16)
    expt = SB("expt", [128, 8192], BF16)
    ebuf0 = SB("ebuf0", [128, 512], BF16)
    ebuf1 = SB("ebuf1", [128, 512], BF16)
    sobuf0 = SB("sobuf0", [128, 512], BF16)
    sobuf1 = SB("sobuf1", [128, 512], BF16)
    htile = SB("htile", [128, HFF], BF16)
    httile = SB("httile", [128, HFF], BF16)
    qtg0 = SB("qtg0", [128, 512], BF16)
    qtg1 = SB("qtg1", [128, 512], BF16)
    ktile0 = SB("ktile0", [128, 128], BF16)
    ktile1 = SB("ktile1", [128, 128], BF16)
    vtile0 = SB("vtile0", [128, 128], BF16)
    vtile1 = SB("vtile1", [128, 128], BF16)
    ot_f32 = SB("ot_f32", [128, 512], F32)

    class ColAlias:
        def __init__(self, t, off):
            self.t = t; self.off = off
        def __getitem__(self, idx):
            ps, fs = idx
            a = 0 if fs.start is None else fs.start
            bnd = fs.stop
            return self.t[ps, self.off + a: self.off + bnd]

    junk = ColAlias(xg, 0)
    invrq_row = zpad
    pairA = ColAlias(xg, 0)
    pairB = ColAlias(xg, 2048)
    roA = ColAlias(xg, 4096)
    roB = ColAlias(xg, 6144)
    otg = xg
    maskbuf = expt
    row_a = row_a_t
    row_b = row_b_t
    angbuf = ot_f32

    p0 = ctx.enter_context(nc.psum_tensor("p0", [128, 512], F32))
    p1 = ctx.enter_context(nc.psum_tensor("p1", [128, 512], F32))
    p2 = ctx.enter_context(nc.psum_tensor("p2", [128, 512], F32))
    p3 = ctx.enter_context(nc.psum_tensor("p3", [128, 512], F32))
    z0 = ctx.enter_context(nc.psum_tensor("z0", [128, 512], F32))
    z1 = ctx.enter_context(nc.psum_tensor("z1", [128, 512], F32))
    z2 = ctx.enter_context(nc.psum_tensor("z2", [128, 512], F32))
    z3 = ctx.enter_context(nc.psum_tensor("z3", [128, 512], F32))
    ZP = [z0, z1, z2, z3]
    PP = [p0, p1]
    PQ = [p2, p3]

    qL = Sem(ctx.enter_context(nc.semaphore("qL")))
    qX = Sem(ctx.enter_context(nc.semaphore("qX")))
    qG = Sem(ctx.enter_context(nc.semaphore("qG")))
    eT = Sem(ctx.enter_context(nc.semaphore("eT")))
    eV = Sem(ctx.enter_context(nc.semaphore("eV")))
    eS = Sem(ctx.enter_context(nc.semaphore("eS")))
    eG = Sem(ctx.enter_context(nc.semaphore("eG")))
    ALLSEMS = [qL, qX, qG, eT, eV, eS, eG]

    blk = ctx.enter_context(nc.Block())

    def fence():
        snap = [(s.h, s.n) for s in ALLSEMS]
        for eng in (blk.sync, blk.tensor, blk.vector, blk.scalar, blk.gpsimd):
            @eng
            def _(e, snap=snap):
                for h, n in snap:
                    if n:
                        e.wait_ge(h, n)

    def replicate_row(rhs_tile, rhs_off, dest, dest_off, W, lhs=None, rhs_f32=False):
        lhs = lhs if lhs is not None else e0
        nsl = (W + 511) // 512
        for i in range(nsl):
            w = min(512, W - i * 512)
            pb = PP[i % 2]
            @blk.tensor
            def _(t, i=i, w=w, pb=pb):
                if i >= 2:
                    t.wait_ge(eV.h, eV.n)
                eT.inc(t.matmul(pb[:, 0:w], lhs[:, :],
                                rhs_tile[0:128, rhs_off + i * 512: rhs_off + i * 512 + w],
                                start=True, stop=True))
            @blk.vector
            def _(v, i=i, w=w, pb=pb):
                v.wait_ge(eT.h, eT.n)
                eV.inc(v.tensor_copy(dest[:, dest_off + i * 512: dest_off + i * 512 + w], pb[:, 0:w]))

    @blk.sync
    def _(s):
        qL.inc(s.dma_start(out=e0[:, :], in_=e0_in.ap()), 16)
        qL.inc(s.dma_start(out=e0col[:, :], in_=e0col_in.ap()), 16)
        qL.inc(s.dma_start(out=onescol[:, :], in_=onescol_in.ap()), 16)
        qL.inc(s.dma_start(out=onesmat[:, :], in_=onesmat_in.ap()), 16)
        qL.inc(s.dma_start(out=wnorm[:, :], in_=wnorm_in.ap()), 16)
        qL.inc(s.dma_start(out=e0f[:, :], in_=e0f_in.ap()), 16)
        qL.inc(s.dma_start(out=xt[0:64, :], in_=ksin_in.ap()), 16)
        qL.inc(s.dma_start(out=xt2[0:64, :], in_=kcos_in.ap()), 16)
        qL.inc(s.dma_start(out=row_a[0:1, 0:D], in_=gamma_in.ap()[None, :]), 16)
        qL.inc(s.dma_start(out=row_b[0:1, 0:D], in_=beta_in.ap()[None, :]), 16)
        qL.inc(s.dma_start(out=sinl[:, :], in_=sin_l_in.ap()), 16)
        qL.inc(s.dma_start(out=cosl[:, :], in_=cos_l_in.ap()), 16)
        qL.inc(s.dma_start(out=zpad_inv[:, :], in_=pospad_in.ap()), 16)

    @blk.gpsimd
    def _(g):
        eG.inc(g.memset(zpad[:, :], 0.0))
        eG.inc(g.memset(b1pad[:, :], 0.0))
        eG.inc(g.memset(b2pad[:, :], 0.0))
        eG.inc(g.memset(scalepad[:, :], 0.0))
        eG.inc(g.memset(shiftcol[:, :], -SHIFT))

    @blk.sync
    def _(s):
        s.wait_ge(eG.h, eG.n)
        qL.inc(s.dma_start(out=scalepad[0:1, :], in_=scale_in.ap()[None, :]), 16)
    @blk.scalar
    def _(s):
        s.wait_ge(qL.h, qL.n)
        s.wait_ge(eG.h, eG.n)
        eS.inc(s.copy(invrq_row[0:1, 0:D], row_a[0:1, 0:D]))
    @blk.tensor
    def _(t):
        t.wait_ge(eS.h, eS.n)
    replicate_row(invrq_row, 0, gam_bc, 0, D)
    @blk.scalar
    def _(s):
        s.wait_ge(eV.h, eV.n)
        eS.inc(s.copy(invrq_row[0:1, 0:D], row_b[0:1, 0:D]))
    @blk.tensor
    def _(t):
        t.wait_ge(eS.h, eS.n)
    replicate_row(invrq_row, 0, bet_bc, 0, D)
    @blk.tensor
    def _(t):
        t.wait_ge(qL.h, qL.n)
        eT.inc(t.matmul(p2[:, 0:1], e0f[:, :], scalepad[0:128, 0:1], start=True, stop=True))
    @blk.vector
    def _(v):
        v.wait_ge(eT.h, eT.n)
        eV.inc(v.tensor_copy(scale_bc[:, :], p2[:, 0:1]))
    fence()

    for trig, lhsT_t, ktab in (("sin", sinl, xt), ("cos", cosl, xt2)):
        for q4 in range(4):
            sl = slice(q4 * 512, q4 * 512 + 512)
            @blk.tensor
            def _(t, lhsT_t=lhsT_t, sl=sl):
                t.wait_ge(eV.h, eV.n)
                eT.inc(t.matmul(p2[0:64, :], lhsT_t[:, 0:64], zpad_inv[0:128, sl],
                                start=True, stop=True))
            @blk.vector
            def _(v, ktab=ktab, sl=sl):
                v.wait_ge(eT.h, eT.n)
                v.wait_ge(eS.h, eS.n)
                v.tensor_copy(angbuf[0:64, :], p2[0:64, :])
                v.scalar_tensor_tensor(p3[0:64, :], ktab[0:64, sl], -CW1, angbuf[0:64, :], ALU.mult, ALU.add)
                v.scalar_tensor_tensor(angbuf[0:64, :], ktab[0:64, sl], -CW2, p3[0:64, :], ALU.mult, ALU.add)
                eV.inc(v.scalar_tensor_tensor(p3[0:64, :], ktab[0:64, sl], -CW3, angbuf[0:64, :], ALU.mult, ALU.add))
            @blk.scalar
            def _(s, trig=trig, sl=sl):
                s.wait_ge(eV.h, eV.n)
                dst = sin2 if trig == "sin" else cos2
                s.activation(dst[0:64, sl], p3[0:64, :], AF.Sin)
                eS.inc(s.copy(dst[64:128, sl], dst[0:64, sl]))
    @blk.gpsimd
    def _(g):
        g.wait_ge(eT.h, eT.n)
        eG.inc(g.memset(zpad_inv[:, :], 0.0))
    fence()

    XTD = [xt, xt2]
    for k in range(NK):
        xk = XTD[k % 2]
        @blk.sync
        def _(s, k=k, xk=xk):
            if k >= 2:
                s.wait_ge(eS.h, eS.n)
            qL.inc(s.dma_start(out=xk[:, :], in_=mu_in.ap()[k]), 16)
        @blk.scalar
        def _(s, xk=xk):
            s.wait_ge(qL.h, qL.n)
            eS.inc(s.activation(junk[:, 0:D], xk[:, :], AF.Square))
        @blk.tensor
        def _(t, k=k):
            t.wait_ge(eS.h, eS.n)
            for sn in range(4):
                mm = t.matmul(ZP[sn][0:1, :], onescol[:, :], junk[:, sn * 512:(sn + 1) * 512],
                              start=(k == 0), stop=(k == NK - 1))
                if k == NK - 1:
                    eT.inc(mm)
    @blk.vector
    def _(v):
        v.wait_ge(eT.h, eT.n)
        for sn in range(4):
            v.tensor_copy(row_a[0:1, sn * 512:(sn + 1) * 512], ZP[sn][0:1, :])
        eV.inc(v.tensor_copy(row_b[0:1, 0:1], row_a[0:1, 0:1]))
    @blk.scalar
    def _(s):
        s.wait_ge(eV.h, eV.n)
        eS.inc(s.activation(row_b[0:1, :], row_a[0:1, :], AF.Sqrt))
    @blk.vector
    def _(v):
        v.wait_ge(eS.h, eS.n)
        v.tensor_scalar(row_a[0:1, :], row_b[0:1, :], L2_EPS, None, ALU.max)
        v.drain()
        eV.inc(v.reciprocal(row_b[0:1, :], row_a[0:1, :]))
    @blk.scalar
    def _(s):
        s.wait_ge(eV.h, eV.n)
        eS.inc(s.copy(invrq_row[0:1, 0:ES], row_b[0:1, 0:ES]))
    @blk.tensor
    def _(t):
        t.wait_ge(eS.h, eS.n)
    replicate_row(invrq_row, 0, big0, 0, ES)
    fence()
    MSET = [(ColAlias(xg, 0), ColAlias(xg, 2048)), (ColAlias(xg, 4096), ColAlias(xg, 6144))]
    for k in range(NK):
        a, bb = MSET[k % 2]
        @blk.gpsimd
        def _(g, k=k, a=a, bb=bb):
            if k >= 2:
                g.wait_ge(qG.h, qG.n)
            qG.inc(g.dma_start(out=a[:, 0:ES], in_=mu_in.ap()[k]), 16)
            g.wait_ge(qG.h, qG.n)
            eG.inc(g.tensor_tensor(bb[:, 0:ES], a[:, 0:ES], big0[:, 0:ES], ALU.mult))
            qG.inc(g.dma_start(out=MUN.ap()[k], in_=bb[:, 0:ES]), 16)
    fence()

    EB = [ebuf0, ebuf1]
    SOB = [sobuf0, sobuf1]
    PPALL = [p0, p1, p2, p3]
    def ffn_call(tag, xsrc, xdst, stages="ABCDE"):
        inv2048 = float(1.0 / 2048.0)
        for m in range(NT):
            xa = XTD[m % 2]
            ldq = None
            @blk.sync
            def _(s, m=m, xa=xa):
                s.wait_ge(eS.h, eS.n)
                s.wait_ge(eV.h, eV.n)
                qL.inc(s.dma_start(out=xa[:, :], in_=xsrc(m)), 16)
            @blk.vector
            def _(v, xa=xa):
                v.wait_ge(qL.h, qL.n)
                v.wait_ge(qG.h, qG.n)
                v.wait_ge(eS.h, eS.n)
                v.tensor_reduce(st_s[:, :], xa[:, :], AX.X, ALU.add)
                v.drain()
                eV.inc(v.tensor_scalar(st_m[:, :], st_s[:, :], inv2048, None, ALU.mult))
            @blk.scalar
            def _(s, xa=xa):
                s.wait_ge(eV.h, eV.n)
                s.activation(junk[:, 0:D], xa[:, :], AF.Square, accum_out=st_s[:, :])
                eS.inc(s.activation(st_t[:, :], st_s[:, :], AF.Copy, scale=inv2048))
            @blk.vector
            def _(v):
                v.wait_ge(eS.h, eS.n)
                v.tensor_scalar(st_u[:, :], st_m[:, :], st_m[:, 0:1], st_t[:, 0:1],
                                ALU.mult, ALU.subtract)
                v.drain()
                eV.inc(v.tensor_scalar(st_t[:, :], st_u[:, :], -1.0, LN_EPS, ALU.mult, ALU.add))
            @blk.scalar
            def _(s):
                s.wait_ge(eV.h, eV.n)
                eS.inc(s.activation(st_u[:, :], st_t[:, :], AF.Sqrt))
            @blk.vector
            def _(v):
                v.wait_ge(eS.h, eS.n)
                v.reciprocal(st_t[:, :], st_u[:, :])
                v.drain()
                v.tensor_scalar(st_u[:, :], st_t[:, :], st_t[:, 0:1], -LN_EPS, ALU.mult, ALU.mult)
                v.drain()
                eV.inc(v.tensor_scalar(st_s[:, :], st_u[:, :], 1.0 + LN_EPS, None, ALU.add))
            @blk.scalar
            def _(s):
                s.wait_ge(eV.h, eV.n)
                eS.inc(s.activation(st_u[:, :], st_s[:, :], AF.Sqrt))
            @blk.vector
            def _(v, xa=xa):
                v.wait_ge(eS.h, eS.n)
                v.reciprocal(st_s[:, :], st_u[:, :])
                v.drain()
                v.tensor_scalar(st_u[:, :], st_t[:, :], st_s[:, 0:1], None, ALU.mult)
                v.drain()
                eV.inc(v.tensor_scalar(lnt[:, :], xa[:, :], st_m[:, 0:1], st_u[:, 0:1],
                                       ALU.subtract, ALU.mult))
            @blk.gpsimd
            def _(g):
                g.wait_ge(eV.h, eV.n)
                g.tensor_tensor(xnt[:, :], lnt[:, :], gam_bc[:, :], ALU.mult)
                eG.inc(g.tensor_tensor(lnt[:, :], xnt[:, :], bet_bc[:, :], ALU.add))
            @blk.scalar
            def _(s):
                s.wait_ge(eG.h, eG.n)
                s.activation(junk[:, 0:D], lnt[:, :], AF.Square, accum_out=st_s[:, :])
                eS.inc(s.activation(st_t[:, :], st_s[:, :], AF.Sqrt))
            @blk.vector
            def _(v):
                v.wait_ge(eS.h, eS.n)
                v.tensor_scalar(st_u[:, :], st_t[:, :], L2_EPS, None, ALU.max)
                v.drain()
                v.reciprocal(st_t[:, :], st_u[:, :])
                v.drain()
                v.tensor_scalar(st_u[:, :], st_t[:, :], scale_bc[:, 0:1], None, ALU.mult)
                v.drain()
                eV.inc(v.tensor_scalar(xnt[:, :], lnt[:, :], st_u[:, 0:1], None, ALU.mult))
            @blk.gpsimd
            def _(g, m=m):
                g.wait_ge(eV.h, eV.n)
                qG.inc(g.dma_start(out=XLN.ap()[m * 128:(m + 1) * 128, :], in_=lnt[:, :]), 16)
                qG.inc(g.dma_start(out=XN.ap()[m * 128:(m + 1) * 128, :], in_=xnt[:, :]), 16)
            fence()
        fence()

        if "B" not in stages:
            return
        @blk.sync
        def _(s):
            for k in range(NK):
                qL.inc(s.dma_start(out=big0[:, k * ES:(k + 1) * ES], in_=MUN.ap()[k]), 16)
        ecount = [0]
        for grp in range(4):
            @blk.scalar
            def _(s, grp=grp):
                s.wait_ge(qG.h, qG.n)
                if grp >= 1:
                    s.wait_ge(eT.h, eT.n)
                for k in range(NK):
                    qX.inc(s.dma_start_transpose(
                        out=xg[:, k * 512:(k + 1) * 512],
                        in_=XN.ap()[grp * 512:(grp + 1) * 512, k * 128:(k + 1) * 128]), 16)
            for mi in range(4):
                m = grp * 4 + mi
                for sn in range(4):
                    pb = PP[(m * 4 + sn) % 2]
                    eb = EB[ecount[0] % 2]
                    ecount[0] += 1
                    @blk.tensor
                    def _(t, grp=grp, mi=mi, sn=sn, pb=pb, m=m):
                        t.wait_ge(qL.h, qL.n)
                        t.wait_ge(qX.h, qX.n)
                        t.wait_ge(eS.h, eS.n)
                        for k in range(NK):
                            mm = t.matmul(pb[:, :],
                                          xg[:, k * 512 + mi * 128: k * 512 + mi * 128 + 128],
                                          big0[:, k * ES + sn * 512: k * ES + sn * 512 + 512],
                                          start=(k == 0), stop=(k == NK - 1))
                        eT.inc(mm)
                    @blk.scalar
                    def _(s, pb=pb, eb=eb, sn=sn):
                        s.wait_ge(eT.h, eT.n)
                        s.wait_ge(qG.h, qG.n)
                        s.wait_ge(eT.h, eT.n)
                        s.activation(eb[:, :], pb[:, :], AF.Exp,
                                     accum_out=zr4[:, sn:sn + 1])
                        eS.inc(s.copy(st_u[:, 0:1], zr4[:, sn:sn + 1]))
                    @blk.tensor
                    def _(t, eb=eb, sn=sn, m=m):
                        t.wait_ge(eS.h, eS.n)
                        mm = t.matmul(ZP[sn][0:1, :], onescol[:, :], eb[:, :],
                                      start=(m == 0), stop=(m == NT - 1))
                        if m == NT - 1:
                            eT.inc(mm)
                    @blk.gpsimd
                    def _(g, eb=eb, m=m, sn=sn):
                        g.wait_ge(eS.h, eS.n)
                        qG.inc(g.dma_start(out=EXPL.ap()[m * 128:(m + 1) * 128,
                                                         sn * 512:(sn + 1) * 512],
                                           in_=eb[:, :]), 16)
                fence()
                @blk.vector
                def _(v, m=m):
                    v.wait_ge(eS.h, eS.n)
                    v.tensor_reduce(st_s[:, :], zr4[:, :], AX.X, ALU.add)
                    v.drain()
                    v.reciprocal(st_t[:, :], st_s[:, :])
                    v.drain()
                    eV.inc(v.tensor_scalar(zrow_ih[:, m:m + 1], st_t[:, :], 0.5, None, ALU.mult))
        @blk.vector
        def _(v):
            v.wait_ge(eT.h, eT.n)
            for sn in range(4):
                v.tensor_copy(row_a[0:1, sn * 512:(sn + 1) * 512], ZP[sn][0:1, :])
            v.drain()
            eV.inc(v.reciprocal(row_b[0:1, :], row_a[0:1, :]))
        @blk.scalar
        def _(s):
            s.wait_ge(eV.h, eV.n)
            s.copy(zpad[0:1, 0:ES], row_a[0:1, 0:ES])
            eS.inc(s.copy(zpad_inv[0:1, 0:ES], row_b[0:1, 0:ES]))
        @blk.tensor
        def _(t):
            t.wait_ge(eS.h, eS.n)
            for e in range(NE):
                mm = t.matmul(p2[:, e:e + 1], zpad_inv[:, e * 128:(e + 1) * 128],
                              e0col[:, :], start=True, stop=True)
            eT.inc(mm)
        @blk.vector
        def _(v):
            v.wait_ge(eT.h, eT.n)
            eV.inc(v.tensor_copy(invzc_cols[:, :], p2[:, 0:NE]))
        fence()

        if "C" not in stages:
            return
        @blk.sync
        def _(s):
            for m in range(NT):
                qL.inc(s.dma_start(out=big0[:, m * D:(m + 1) * D],
                                   in_=XLN.ap()[m * 128:(m + 1) * 128, :]), 16)
        for sn in range(4):
            for quad in range(4):
                dbs = [quad * 4 + i for i in range(4)]
                snapC = {}
                for m in range(NT):
                    eb = EB[m % 2]
                    @blk.sync
                    def _(s, m=m, sn=sn, eb=eb):
                        if m >= 2:
                            s.wait_ge(eT.h, snapC[m - 2])
                        qL.inc(s.dma_start(out=eb[:, :],
                                           in_=EXPL.ap()[m * 128:(m + 1) * 128,
                                                         sn * 512:(sn + 1) * 512]), 16)
                    @blk.tensor
                    def _(t, m=m, sn=sn, eb=eb, dbs=dbs):
                        t.wait_ge(qL.h, qL.n)
                        if m == 0:
                            t.wait_ge(eV.h, eV.n)
                        for i, db in enumerate(dbs):
                            mm = t.matmul(PPALL[i][:, :],
                                          big0[:, m * D + db * 128: m * D + db * 128 + 128],
                                          eb[:, :],
                                          start=(m == 0), stop=(m == NT - 1))
                        eT.inc(mm)
                    snapC[m] = eT.n
                @blk.vector
                def _(v, sn=sn, dbs=dbs):
                    v.wait_ge(eT.h, eT.n)
                    for i, db in enumerate(dbs):
                        eV.inc(v.tensor_copy(xg[:, i * 512:(i + 1) * 512], PPALL[i][:, :]))
                @blk.gpsimd
                def _(g, sn=sn, dbs=dbs):
                    g.wait_ge(eV.h, eV.n)
                    for i, db in enumerate(dbs):
                        qG.inc(g.dma_start(out=SIT.ap()[db][:, sn * 512:(sn + 1) * 512],
                                           in_=xg[:, i * 512:(i + 1) * 512]), 16)
                fence()
        fence()

        if "D" not in stages:
            return
        W2B = [w2buf0, w2buf1]
        for e in range(NE):
            @blk.gpsimd
            def _(g, e=e):
                g.wait_ge(eT.h, eT.n)
                qG.inc(g.dma_start(out=wbuf[:, 0:2048].rearrange("p (k f) -> p k f", k=4),
                                   in_=w1_in.ap()[e, 0:4, :, :].rearrange("k p f -> p k f")), 16)
                qG.inc(g.dma_start(out=b1pad[0:1, :], in_=b1_in.ap()[e][None, :]), 16)
                qG.inc(g.dma_start(out=b2pad[0:1, :], in_=b2_in.ap()[e][None, :]), 16)
            @blk.sync
            def _(s, e=e):
                s.wait_ge(eT.h, eT.n)
                for k in range(NK):
                    qL.inc(s.dma_start(out=expt[:, k * 128:(k + 1) * 128],
                                       in_=SIT.ap()[k][:, e * 128:(e + 1) * 128]), 16)
            snapD = {}
            for qi in range(4):
                @blk.gpsimd
                def _(g, e=e, qi=qi):
                    if qi == 0:
                        return
                    if qi >= 2:
                        g.wait_ge(eT.h, snapD[qi - 2])
                    qG.inc(g.dma_start(
                        out=wbuf[:, (qi % 2) * 2048:(qi % 2) * 2048 + 2048].rearrange("p (k f) -> p k f", k=4),
                        in_=w1_in.ap()[e, 4 * qi:4 * qi + 4, :, :].rearrange("k p f -> p k f")), 16)
                @blk.tensor
                def _(t, e=e, qi=qi):
                    t.wait_ge(qG.h, qG.n)
                    if qi == 0:
                        t.wait_ge(qL.h, qL.n)
                        t.wait_ge(eS.h, eS.n)
                        t.matmul(p0[:, :], zpad[:, e * 128:(e + 1) * 128], b1pad[:, :],
                                 start=True, stop=False)
                    for kk in range(4):
                        k = 4 * qi + kk
                        mm = t.matmul(p0[:, :], expt[:, k * 128:(k + 1) * 128],
                                      wbuf[:, (qi % 2) * 2048 + kk * 512:(qi % 2) * 2048 + kk * 512 + 512],
                                      start=False, stop=(k == NK - 1))
                    eT.inc(mm)
                snapD[qi] = eT.n
            @blk.scalar
            def _(s, e=e):
                s.wait_ge(eT.h, eT.n)
                s.wait_ge(qX.h, qX.n)
                eS.inc(s.activation(htile[:, :], p0[:, :], AF.Gelu,
                                    scale=invzc_cols[:, e:e + 1]))
            @blk.gpsimd
            def _(g):
                g.wait_ge(eS.h, eS.n)
                g.wait_ge(qX.h, qX.n)
                qG.inc(g.dma_start(out=HSCR.ap()[:, :], in_=htile[:, :]), 16)
            @blk.scalar
            def _(s):
                s.wait_ge(qG.h, qG.n)
                for hc in range(4):
                    qX.inc(s.dma_start_transpose(
                        out=httile[:, hc * 128:(hc + 1) * 128],
                        in_=HSCR.ap()[0:128, hc * 128:(hc + 1) * 128]), 16)
            for dn in range(4):
                wb = W2B[dn % 2]
                @blk.gpsimd
                def _(g, e=e, dn=dn, wb=wb):
                    if dn >= 2 or e > 0:
                        g.wait_ge(eT.h, eT.n)
                    qG.inc(g.dma_start(
                        out=wb[:, :].rearrange("p (h f) -> p h f", h=4),
                        in_=w2_in.ap()[e, :, :, dn * 512:(dn + 1) * 512]
                        .rearrange("h p f -> p h f")), 16)
                @blk.tensor
                def _(t, e=e, dn=dn, wb=wb):
                    t.wait_ge(qG.h, qG.n)
                    t.wait_ge(qX.h, qX.n)
                    if dn == 0:
                        t.wait_ge(eV.h, eV.n)
                    t.matmul(p1[:, :], e0[:, :], b2pad[:, dn * 512:(dn + 1) * 512],
                             start=True, stop=False)
                    for hc in range(4):
                        mm = t.matmul(p1[:, :], httile[:, hc * 128:(hc + 1) * 128],
                                      wb[:, hc * 512:(hc + 1) * 512],
                                      start=False, stop=(hc == 3))
                    eT.inc(mm)
                @blk.vector
                def _(v, e=e, dn=dn):
                    v.wait_ge(eT.h, eT.n)
                    v.wait_ge(qG.h, qG.n)
                    eV.inc(v.tensor_copy(qsq[:, :], p1[:, :]))
                @blk.gpsimd
                def _(g, e=e, dn=dn):
                    g.wait_ge(eV.h, eV.n)
                    qG.inc(g.dma_start(out=SOT.ap()[e][:, dn * 512:(dn + 1) * 512],
                                       in_=qsq[:, :]), 16)
            fence()
        fence()

        if "E" not in stages:
            return
        @blk.sync
        def _(s):
            s.wait_ge(qG.h, qG.n)
            for m in range(NT):
                qL.inc(s.dma_start(out=xdst(m), in_=xsrc(m)), 16)
        for grp in range(4):
            @blk.scalar
            def _(s, grp=grp):
                s.wait_ge(eT.h, eT.n)
                for sc in range(NK):
                    qX.inc(s.dma_start_transpose(
                        out=expt[:, sc * 512:(sc + 1) * 512],
                        in_=EXPL.ap()[grp * 512:(grp + 1) * 512, sc * 128:(sc + 1) * 128]), 16)
            for dn in range(4):
                snapE = {}
                for sc in range(NK):
                    sob = SOB[sc % 2]
                    @blk.sync
                    def _(s, grp=grp, dn=dn, sc=sc, sob=sob):
                        s.wait_ge(qL.h, qL.n)
                        if sc >= 2:
                            s.wait_ge(eT.h, snapE[sc - 2])
                        qL.inc(s.dma_start(out=sob[:, :],
                                           in_=SOT.ap()[sc][:, dn * 512:(dn + 1) * 512]), 16)
                    @blk.tensor
                    def _(t, grp=grp, dn=dn, sc=sc, sob=sob):
                        t.wait_ge(qL.h, qL.n)
                        t.wait_ge(qX.h, qX.n)
                        if sc == 0:
                            t.wait_ge(eV.h, eV.n)
                        for mi in range(4):
                            mm = t.matmul(PPALL[mi][:, :],
                                          expt[:, sc * 512 + mi * 128: sc * 512 + mi * 128 + 128],
                                          sob[:, :],
                                          start=(sc == 0), stop=(sc == NK - 1))
                        eT.inc(mm)
                    snapE[sc] = eT.n
                for mi in range(4):
                    m = grp * 4 + mi
                    @blk.vector
                    def _(v, m=m, mi=mi):
                        v.wait_ge(eT.h, eT.n)
                        v.wait_ge(qG.h, qG.n)
                        eV.inc(v.tensor_scalar(ot_f32[:, :], PPALL[mi][:, :],
                                               zrow_ih[:, m:m + 1], None, ALU.mult))
                    @blk.gpsimd
                    def _(g, m=m, dn=dn):
                        g.wait_ge(eV.h, eV.n)
                        qG.inc(g.dma_start(out=xdst(m, dn * 512, (dn + 1) * 512),
                                           in_=ot_f32[:, :], accum_op=ALU.add), 16)
                fence()
        fence()

    def _sl(t):
        def f(m, c0=0, c1=D):
            return t.ap()[m * 128:(m + 1) * 128, c0:c1]
        return f
    xsrc_input = _sl(x_in)
    xsrc_X2 = _sl(X2); xdst_X2 = _sl(X2)
    xsrc_X3 = _sl(X3); xdst_X3 = _sl(X3)
    xdst_out = _sl(out)

    if EMIT_FFN1:
        ffn_call("ffn1", xsrc_input, xdst_X2, stages=FFN_STAGES)

    if EMIT_ATTN:
        inv2048 = float(1.0 / 2048.0)
        for m in range(NT):
            xa = XTD[m % 2]
            @blk.sync
            def _(s, m=m, xa=xa):
                s.wait_ge(eS.h, eS.n)
                s.wait_ge(eV.h, eV.n)
                qL.inc(s.dma_start(out=xa[:, :], in_=xsrc_X2(m)), 16)
            @blk.vector
            def _(v, xa=xa):
                v.wait_ge(qL.h, qL.n)
                v.wait_ge(qG.h, qG.n)
                v.wait_ge(eS.h, eS.n)
                v.tensor_reduce(st_s[:, :], xa[:, :], AX.X, ALU.add)
                v.drain()
                eV.inc(v.tensor_scalar(st_m[:, :], st_s[:, :], inv2048, None, ALU.mult))
            @blk.scalar
            def _(s, xa=xa):
                s.wait_ge(eV.h, eV.n)
                s.activation(junk[:, 0:D], xa[:, :], AF.Square, accum_out=st_s[:, :])
                eS.inc(s.activation(st_t[:, :], st_s[:, :], AF.Copy, scale=inv2048))
            @blk.vector
            def _(v):
                v.wait_ge(eS.h, eS.n)
                v.tensor_scalar(st_u[:, :], st_m[:, :], st_m[:, 0:1], st_t[:, 0:1],
                                ALU.mult, ALU.subtract)
                v.drain()
                eV.inc(v.tensor_scalar(st_t[:, :], st_u[:, :], -1.0, LN_EPS, ALU.mult, ALU.add))
            @blk.scalar
            def _(s):
                s.wait_ge(eV.h, eV.n)
                eS.inc(s.activation(st_u[:, :], st_t[:, :], AF.Sqrt))
            @blk.vector
            def _(v, xa=xa):
                v.wait_ge(eS.h, eS.n)
                v.reciprocal(st_t[:, :], st_u[:, :])
                v.drain()
                eV.inc(v.tensor_scalar(lnt[:, :], xa[:, :], st_m[:, 0:1], st_t[:, 0:1],
                                       ALU.subtract, ALU.mult))
            @blk.gpsimd
            def _(g, m=m):
                g.wait_ge(eV.h, eV.n)
                qG.inc(g.dma_start(out=XLN2.ap()[m * 128:(m + 1) * 128, :], in_=lnt[:, :]), 16)
            fence()
        fence()
        @blk.scalar
        def _(s):
            for k in range(NK):
                for tn in range(4):
                    qX.inc(s.dma_start_transpose(
                        out=big0[:, k * S + tn * 512: k * S + tn * 512 + 512],
                        in_=XLN2.ap()[tn * 512:(tn + 1) * 512, k * 128:(k + 1) * 128]), 16)
        fence()

        for fb in range(32):
            half = fb % 2
            isq = fb < 16
            first_of_kind = (fb % 16) == 0
            last_of_kind = (fb % 16) == 15
            @blk.gpsimd
            def _(g, fb=fb, half=half):
                g.wait_ge(eT.h, eT.n)
                qG.inc(g.dma_start(out=wbuf[:, half * 2048:half * 2048 + 2048],
                                   in_=wqk_in.ap()[fb].rearrange("p k f -> p (k f)")), 16)
            ptile = pairA if fb % 2 == 0 else pairB
            for tn in range(4):
                pb = PP[tn % 2]
                @blk.tensor
                def _(t, fb=fb, half=half, tn=tn, pb=pb):
                    t.wait_ge(qG.h, qG.n)
                    t.wait_ge(qX.h, qX.n)
                    t.wait_ge(eS.h, eS.n)
                    t.wait_ge(eV.h, eV.n)
                    for k in range(NK):
                        mm = t.matmul(pb[:, :],
                                      wbuf[:, half * 2048 + k * 128: half * 2048 + k * 128 + 128],
                                      big0[:, k * S + tn * 512: k * S + tn * 512 + 512],
                                      start=(k == 0), stop=(k == NK - 1))
                    eT.inc(mm)
                @blk.scalar
                def _(s, fb=fb, tn=tn, pb=pb, ptile=ptile):
                    s.wait_ge(eT.h, eT.n)
                    s.wait_ge(eV.h, eV.n)
                    s.activation(qsq[:, :], pb[:, :], AF.Square)
                    eS.inc(s.activation(ptile[:, tn * 512:(tn + 1) * 512], pb[:, :],
                                        AF.Copy, scale=wnorm[:, fb:fb + 1]))
                @blk.tensor
                def _(t, fb=fb, tn=tn, first_of_kind=first_of_kind, last_of_kind=last_of_kind):
                    t.wait_ge(eS.h, eS.n)
                    mm = t.matmul(ZP[tn][0:128, :], onesmat[:, :], qsq[:, :],
                                  start=first_of_kind, stop=last_of_kind)
                    if last_of_kind:
                        eT.inc(mm)
            if fb % 2 == 1:
                @blk.vector
                def _(v):
                    v.wait_ge(eS.h, eS.n)
                    v.wait_ge(qG.h, qG.n)
                    eV.inc(v.tensor_tensor(xt[:, :], pairA[:, 0:S], cos2[:, :], ALU.mult))
                @blk.gpsimd
                def _(g):
                    g.wait_ge(eS.h, eS.n)
                    g.wait_ge(qG.h, qG.n)
                    eG.inc(g.tensor_tensor(xt2[:, :], pairB[:, 0:S], sin2[:, :], ALU.mult))
                @blk.vector
                def _(v):
                    v.wait_ge(eG.h, eG.n)
                    eV.inc(v.tensor_tensor(roA[:, 0:S], xt[:, :], xt2[:, :], ALU.subtract))
                    eV.inc(v.tensor_tensor(xt[:, :], pairB[:, 0:S], cos2[:, :], ALU.mult))
                @blk.gpsimd
                def _(g):
                    g.wait_ge(eV.h, eV.n)
                    eG.inc(g.tensor_tensor(xt2[:, :], pairA[:, 0:S], sin2[:, :], ALU.mult))
                @blk.vector
                def _(v):
                    v.wait_ge(eG.h, eG.n)
                    eV.inc(v.tensor_tensor(roB[:, 0:S], xt[:, :], xt2[:, :], ALU.add))
                fence()
                @blk.gpsimd
                def _(g, fb=fb, isq=isq):
                    g.wait_ge(eV.h, eV.n)
                    dstT = QT if isq else KT
                    pr = (fb % 16) // 2
                    h0 = 2 * pr; h1 = h0 + 1
                    qG.inc(g.dma_start(out=dstT.ap()[h0 * HD + 0: h0 * HD + 64, :],
                                       in_=roA[0:64, 0:S]), 16)
                    qG.inc(g.dma_start(out=dstT.ap()[h1 * HD + 0: h1 * HD + 64, :],
                                       in_=roA[64:128, 0:S]), 16)
                    qG.inc(g.dma_start(out=dstT.ap()[h0 * HD + 64: h0 * HD + 128, :],
                                       in_=roB[0:64, 0:S]), 16)
                    qG.inc(g.dma_start(out=dstT.ap()[h1 * HD + 64: h1 * HD + 128, :],
                                       in_=roB[64:128, 0:S]), 16)
            if last_of_kind:
                @blk.vector
                def _(v):
                    v.wait_ge(eT.h, eT.n)
                    for tn in range(4):
                        v.tensor_copy(row_a[0:1, tn * 512:(tn + 1) * 512], ZP[tn][0:1, :])
                    v.drain()
                    eV.inc(v.tensor_scalar(row_b[0:1, :], row_a[0:1, :], inv2048, RMS_EPS,
                                           ALU.mult, ALU.add))
                @blk.scalar
                def _(s):
                    s.wait_ge(eV.h, eV.n)
                    eS.inc(s.activation(row_a[0:1, :], row_b[0:1, :], AF.Sqrt))
                @blk.vector
                def _(v):
                    v.wait_ge(eS.h, eS.n)
                    eV.inc(v.reciprocal(row_b[0:1, :], row_a[0:1, :]))
                if isq:
                    @blk.scalar
                    def _(s):
                        s.wait_ge(eV.h, eV.n)
                        eS.inc(s.activation(row_a[0:1, :], row_b[0:1, :], AF.Copy, scale=SC_HI))
                    @blk.vector
                    def _(v):
                        v.wait_ge(eS.h, eS.n)
                        eV.inc(v.scalar_tensor_tensor(zpad_inv[0:1, 0:S], row_b[0:1, :], SC_LO,
                                                      row_a[0:1, :], ALU.mult, ALU.add))
                    @blk.gpsimd
                    def _(g):
                        g.wait_ge(eV.h, eV.n)
                        qG.inc(g.dma_start(out=IRQ.ap()[0:1, :], in_=zpad_inv[0:1, 0:S]), 16)
                else:
                    @blk.scalar
                    def _(s):
                        s.wait_ge(eV.h, eV.n)
                        s.wait_ge(qG.h, qG.n)
                        eS.inc(s.copy(zpad_inv[0:1, 0:S], row_b[0:1, 0:S]))
                    @blk.tensor
                    def _(t):
                        t.wait_ge(eS.h, eS.n)
                        for kb in range(NT):
                            mm = t.matmul(p2[:, kb:kb + 1], zpad_inv[:, kb * 128:(kb + 1) * 128],
                                          e0col[:, :], start=True, stop=True)
                        eT.inc(mm)
                    @blk.vector
                    def _(v):
                        v.wait_ge(eT.h, eT.n)
                        eV.inc(v.tensor_copy(invrk_cols[:, :], p2[:, 0:NT]))
        fence()

        for fn8 in range(8):
            @blk.gpsimd
            def _(g, fn8=fn8):
                g.wait_ge(eT.h, eT.n)
                qG.inc(g.dma_start(out=wbuf[:, 0:4096].rearrange("p (k f) -> p k f", k=NK),
                                   in_=wv_in.ap()[:, :, fn8 * 256:(fn8 + 1) * 256]
                                   .rearrange("k p f -> p k f")), 16)
            for m in range(NT):
                pb = PP[m % 2]
                eb = EB[m % 2]
                @blk.tensor
                def _(t, fn8=fn8, m=m, pb=pb):
                    t.wait_ge(qG.h, qG.n)
                    t.wait_ge(eV.h, eV.n)
                    for k in range(NK):
                        mm = t.matmul(pb[:, 0:256],
                                      big0[:, k * S + m * 128: k * S + m * 128 + 128],
                                      wbuf[:, k * 256:(k + 1) * 256],
                                      start=(k == 0), stop=(k == NK - 1))
                    eT.inc(mm)
                @blk.vector
                def _(v, pb=pb, eb=eb):
                    v.wait_ge(eT.h, eT.n)
                    v.wait_ge(qG.h, qG.n)
                    eV.inc(v.tensor_copy(eb[:, 0:256], pb[:, 0:256]))
                @blk.gpsimd
                def _(g, fn8=fn8, m=m, eb=eb):
                    g.wait_ge(eV.h, eV.n)
                    qG.inc(g.dma_start(out=VT.ap()[m * 128:(m + 1) * 128,
                                                   fn8 * 256:(fn8 + 1) * 256],
                                       in_=eb[:, 0:256]), 16)
            fence()
        fence()

        @blk.sync
        def _(s):
            for m in range(NT):
                qL.inc(s.dma_start(out=xdst_X3(m), in_=xsrc_X2(m)), 16)
        KTIL = [ktile0, ktile1]
        VTIL = [vtile0, vtile1]
        MEB = [qsq, htile]
        for g_ in G_ORDER:
            vblocks = [j for (gg, j) in mask_valid if gg == g_]
            mask_slots = {}
            @blk.sync
            def _(s, g_=g_, vblocks=vblocks):
                s.wait_ge(eV.h, eV.n)
                for i, j in enumerate(vblocks):
                    idx = mask_sched.get((g_, j), N_MASKS - 1)
                    mask_slots[j] = i
                    qL.inc(s.dma_start(out=expt[:, i * 512:(i + 1) * 512],
                                       in_=masks_in.ap()[idx]), 16)
            @blk.sync
            def _(s, g_=g_):
                s.wait_ge(eT.h, eT.n)
                qL.inc(s.dma_start(out=zpad_inv[0:1, 512:1024],
                                   in_=IRQ.ap()[0:1, g_ * 512:(g_ + 1) * 512]), 16)
            @blk.tensor
            def _(t):
                t.wait_ge(qL.h, qL.n)
                t.wait_ge(eV.h, eV.n)
                eT.inc(t.matmul(p2[:, :], e0f[:, :], zpad_inv[0:128, 512:1024],
                                start=True, stop=True))
            @blk.vector
            def _(v):
                v.wait_ge(eT.h, eT.n)
                eV.inc(v.tensor_copy(invrq_g[:, :], p2[:, :]))
            for h in range(NH):
                pq = PQ[h % 2]
                @blk.sync
                def _(s, g_=g_, h=h):
                    s.wait_ge(eV.h, eV.n)
                    qL.inc(s.dma_start(out=qtg0[:, :],
                                       in_=QT.ap()[h * HD:(h + 1) * HD,
                                                   g_ * 512:(g_ + 1) * 512]), 16)
                @blk.vector
                def _(v, g_=g_):
                    v.wait_ge(qL.h, qL.n)
                    v.wait_ge(eT.h, eT.n)
                    eV.inc(v.tensor_tensor(qtg1[:, :], qtg0[:, :],
                                           invrq_g[:, :], ALU.mult))
                fence()
                for idx, j in enumerate(vblocks):
                    kb = g_ * 4 + j
                    pb = PP[idx % 2]
                    kt = KTIL[idx % 2]
                    vt = VTIL[idx % 2]
                    meb = MEB[idx % 2]
                    eb = EB[idx % 2]
                    masked = (g_, j) in mask_sched
                    @blk.sync
                    def _(s, h=h, kb=kb, kt=kt, vt=vt):
                        if True:
                            s.wait_ge(eT.h, eT.n)
                        qL.inc(s.dma_start(out=kt[:, :],
                                           in_=KT.ap()[h * HD:(h + 1) * HD,
                                                       kb * 128:(kb + 1) * 128]), 16)
                        qL.inc(s.dma_start(out=vt[:, :],
                                           in_=VT.ap()[kb * 128:(kb + 1) * 128,
                                                       h * HD:(h + 1) * HD]), 16)
                    @blk.tensor
                    def _(t, pb=pb, kt=kt):
                        t.wait_ge(qL.h, qL.n)
                        t.wait_ge(eV.h, eV.n)
                        t.wait_ge(eS.h, eS.n)
                        eT.inc(t.matmul(pb[:, :], kt[:, :], qtg1[:, :], start=True, stop=True))
                    @blk.scalar
                    def _(s, pb=pb, eb=eb, kb=kb):
                        s.wait_ge(eT.h, eT.n)
                        s.wait_ge(eT.h, eT.n)
                        eS.inc(s.activation(eb[:, :], pb[:, :], AF.Exp,
                                            scale=invrk_cols[:, kb:kb + 1], bias=shiftcol[:, 0:1]))
                    use = meb
                    @blk.vector
                    def _(v, eb=eb, meb=meb, j=j, g_=g_):
                        v.wait_ge(eS.h, eS.n)
                        v.wait_ge(eT.h, eT.n)
                        v.wait_ge(qL.h, qL.n)
                        sl = mask_slots[j]
                        eV.inc(v.tensor_tensor(meb[:, :], eb[:, :],
                                               expt[:, sl * 512:(sl + 1) * 512], ALU.mult))
                    @blk.tensor
                    def _(t, use=use, vt=vt, pq=pq, idx=idx, n=len(vblocks)):
                        t.wait_ge(eV.h, eV.n)
                        t.matmul(pq[:, :], vt[:, :], use[:, :],
                                 start=(idx == 0), stop=(idx == n - 1))
                        mm = t.matmul(z0[0:128, :], onesmat[:, :], use[:, :],
                                      start=(idx == 0), stop=(idx == n - 1))
                        eT.inc(mm)
                    fence()
                @blk.vector
                def _(v, pq=pq, h=h):
                    v.wait_ge(eT.h, eT.n)
                    v.tensor_copy(ot_f32[:, :], pq[:, :])
                    v.reciprocal(invz_bc[:, :], z0[0:128, :])
                    v.drain()
                    eV.inc(v.tensor_tensor(otg[:, h * 512:(h + 1) * 512], ot_f32[:, :],
                                           invz_bc[:, :], ALU.mult))
                fence()
            fence()
            for dn in range(4):
                snapO = {}
                for fc in range(NK):
                    wob = SOB[fc % 2]
                    @blk.gpsimd
                    def _(g, fc=fc, dn=dn, wob=wob):
                        if fc >= 2:
                            g.wait_ge(eT.h, snapO[fc - 2])
                        else:
                            g.wait_ge(eT.h, eT.n)
                        qG.inc(g.dma_start(out=wob[:, :],
                                           in_=wo_in.ap()[fc][:, dn * 512:(dn + 1) * 512]), 16)
                    @blk.tensor
                    def _(t, fc=fc, dn=dn, wob=wob, g_=g_):
                        t.wait_ge(qG.h, qG.n)
                        if fc == 0:
                            t.wait_ge(eV.h, eV.n)
                        for tb in range(4):
                            mm = t.matmul(PPALL[tb][:, :],
                                          otg[:, fc * 512 + tb * 128: fc * 512 + tb * 128 + 128],
                                          wob[:, :], start=(fc == 0), stop=(fc == NK - 1))
                        eT.inc(mm)
                    snapO[fc] = eT.n
                for tb in range(4):
                    m = g_ * 4 + tb
                    @blk.vector
                    def _(v, tb=tb):
                        v.wait_ge(eT.h, eT.n)
                        v.wait_ge(qG.h, qG.n)
                        eV.inc(v.tensor_copy(ot_f32[:, :], PPALL[tb][:, :]))
                    @blk.gpsimd
                    def _(g, m=m, dn=dn):
                        g.wait_ge(eV.h, eV.n)
                        qG.inc(g.dma_start(out=xdst_X3(m, dn * 512, (dn + 1) * 512),
                                           in_=ot_f32[:, :], accum_op=ALU.add), 16)
            fence()
        fence()
        if EMIT_FFN2:
            ffn_call("ffn2", xsrc_X3, xdst_out, stages=FFN_STAGES)
    else:
        @blk.sync
        def _(s):
            for m in range(NT):
                qL.inc(s.dma_start(out=out.ap()[m * 128:(m + 1) * 128, :],
                                   in_=X2.ap()[m * 128:(m + 1) * 128, :]), 16)

    fence()
    @blk.sync
    def _(s):
        for name, (src, dst) in tap_out.items():
            qL.inc(s.dma_start(out=dst.ap(), in_=src.ap()), 16)
    sbuf_tap_map = {"gam_bc": (gam_bc, BF16), "bet_bc": (bet_bc, BF16), "scale_bc": (scale_bc, F32),
                    "sin2": (sin2, BF16), "cos2": (cos2, BF16), "invrq_g": (invrq_g, F32),
                    "zpad": (zpad, BF16), "zpad_inv": (zpad_inv, F32), "invzc_cols": (invzc_cols, F32),
                    "zrow_ih": (zrow_ih, F32), "invrk_cols": (invrk_cols, F32), "invz_bc": (invz_bc, F32), "otg": (xg, BF16), "qtg1": (qtg1, BF16), "ebuf0": (ebuf0, BF16), "ot_f32": (ot_f32, F32), "row_a": (row_a_t, F32), "htile": (htile, BF16), "httile": (httile, BF16), "qtg1b": (qtg1, BF16), "qtg0b": (qtg0, BF16), "st_m": (st_m, F32), "st_s": (st_s, F32), "st_t": (st_t, F32), "st_u": (st_u, F32), "lnt": (lnt, BF16), "xnt": (xnt, BF16), "xt": (xt, F32), "xt2": (xt2, F32)}
    @blk.sync
    def _(s):
        for name in SBUF_TAPS:
            t, dt = sbuf_tap_map[name]
            shp = [t.shape[0], t.shape[1]] if hasattr(t, "shape") else None
            o = nc.dram_tensor("stap_" + name, list(t.shape), dt, kind="ExternalOutput")
            qL.inc(s.dma_start(out=o.ap(), in_=t[:, :]), 16)
    fence()





from concourse.bass_utils import run_bass_kernel_spmd


_NC_CACHE = {}


def kernel(**inputs):
    if "nc" not in _NC_CACHE:
        _NC_CACHE["nc"] = build_program(debug_taps=())
    nc = _NC_CACHE["nc"]
    consts, _, _ = host_consts(N_MASKS)
    w = host_weights(inputs)
    x_all = np.ascontiguousarray(np.asarray(inputs["L0_attn_hidden_states"], dtype=np.float32))
    in_maps = []
    for r in range(8):
        im = {"x": np.ascontiguousarray(x_all[r % 4])}
        im.update(w)
        im.update(consts)
        in_maps.append(im)
    res = run_bass_kernel_spmd(nc, in_maps, core_ids=list(range(8)))
    out = np.stack([np.asarray(res.results[b]["out"], dtype=np.float32) for b in range(4)], axis=0)
    return out


# version 07 (final)
# speedup vs baseline: 1.1644x; 1.1644x over previous
import numpy as np
import ml_dtypes
from contextlib import ExitStack

import concourse.bass as bass
import concourse.mybir as mybir

F32 = mybir.dt.float32
BF16 = mybir.dt.bfloat16
AF = mybir.ActivationFunctionType
ALU = mybir.AluOpType
AX = mybir.AxisListType
bf = ml_dtypes.bfloat16

D = 2048; S = 2048; NH = 16; HD = 128
NE = 16; NSLOT = 128; ES = NE * NSLOT; HFF = 512
WINDOW = 1024
NT = S // 128; NK = D // 128
LN_EPS = 1e-5; RMS_EPS = 1e-6; L2_EPS = 1e-12
SHIFT = 12.0

TWO_PI = 2.0 * np.pi
CW1 = float(np.float32(np.ldexp(np.round(np.ldexp(TWO_PI, 10)), -10)))
CW2 = float(np.float32(np.ldexp(np.round(np.ldexp(TWO_PI - CW1, 22)), -22)))
CW3 = float(np.float32(TWO_PI - CW1 - CW2))
SC_HI = float(np.float32(1.0 / np.sqrt(np.float64(HD))))
SC_LO = float(np.float32(1.0 / np.sqrt(np.float64(HD)) - np.float64(SC_HI)))


def host_consts(n_masks_pad):
    c = {}
    e0 = np.zeros((128, 128), dtype=bf); e0[0, :] = bf(1.0)
    c["e0"] = e0
    c["onescol"] = np.ones((128, 1), dtype=bf)
    c["onesmat"] = np.ones((128, 128), dtype=bf)
    e0col = np.zeros((128, 1), dtype=np.float32); e0col[0, 0] = 1.0
    c["e0col_f32"] = e0col
    e0f = np.zeros((128, 128), dtype=np.float32); e0f[0, :] = 1.0
    c["e0f"] = e0f
    inv_freq64 = 1.0 / (10000.0 ** (np.arange(0, HD, 2, dtype=np.float64) / HD))
    hi = inv_freq64.astype(np.float32)
    lo = (inv_freq64 - hi.astype(np.float64)).astype(np.float32)
    sin_lhsT = np.zeros((128, 64), dtype=np.float32)
    sin_lhsT[0, :] = hi; sin_lhsT[1, :] = lo
    cos_lhsT = sin_lhsT.copy()
    pih = np.float32(np.pi / 2.0); pil = np.float32(np.float64(np.pi / 2.0) - np.float64(pih))
    cos_lhsT[2, :] = pih; cos_lhsT[3, :] = pil
    c["sin_lhst"] = sin_lhsT; c["cos_lhst"] = cos_lhsT
    pos = np.arange(S, dtype=np.float64)
    pospad = np.zeros((128, S), dtype=np.float32)
    pospad[0, :] = pos; pospad[1, :] = pos; pospad[2, :] = 1.0; pospad[3, :] = 1.0
    c["pospad"] = pospad
    ang_sin = inv_freq64[:, None] * pos[None, :]
    ang_cos = ang_sin + np.pi / 2.0
    c["ksin"] = np.round(ang_sin / TWO_PI).astype(np.float32)
    c["kcos"] = np.round(ang_cos / TWO_PI).astype(np.float32)
    for ang, k in ((ang_sin, c["ksin"]), (ang_cos, c["kcos"])):
        r = ang - k.astype(np.float64) * TWO_PI
        assert np.abs(np.abs(r) - np.pi).min() > 1e-6
    masks, sched, valid = mask_schedule()
    mpad = np.ones((n_masks_pad, 128, 512), dtype=bf)
    mpad[: masks.shape[0]] = masks
    c["masks"] = mpad
    return c, sched, valid


def host_weights(inp):
    w = {}
    Wqkv = np.asarray(inp["L0_attn_Wqkv"])
    def fb_layout(cols):
        return Wqkv[:, cols].reshape(NK, 128, 128).transpose(1, 0, 2).copy()
    blocks = []
    for base in (0, D):
        for pair in range(NH // 2):
            h0, h1 = 2 * pair, 2 * pair + 1
            cA = np.concatenate([base + h0 * HD + np.arange(64), base + h1 * HD + np.arange(64)])
            cB = cA + 64
            blocks.append(fb_layout(cA)); blocks.append(fb_layout(cB))
    w["wqk"] = np.ascontiguousarray(np.stack(blocks))
    w["wv"] = np.ascontiguousarray(Wqkv[:, 2 * D:].reshape(NK, 128, D))
    w["wo"] = np.ascontiguousarray(np.asarray(inp["L0_attn_Wo"]).reshape(NK, 128, D))
    w["mu"] = np.ascontiguousarray(np.asarray(inp["L0_ffn_mu"]).reshape(D, ES).reshape(NK, 128, ES))
    w["w1"] = np.ascontiguousarray(np.asarray(inp["L0_ffn_W1"]).reshape(NE, NK, 128, HFF))
    w["w2"] = np.ascontiguousarray(np.asarray(inp["L0_ffn_W2"]).reshape(NE, 4, 128, D))
    w["b1"] = np.ascontiguousarray(inp["L0_ffn_b1"])
    w["b2"] = np.ascontiguousarray(inp["L0_ffn_b2"])
    w["gamma"] = np.ascontiguousarray(inp["L0_ffn_gamma"])
    w["beta"] = np.ascontiguousarray(inp["L0_ffn_beta"])
    wn = np.zeros((128, 32), dtype=np.float32)
    qw = np.asarray(inp["L0_attn_q_norm_w"]); kw = np.asarray(inp["L0_attn_k_norm_w"])
    i = 0
    for vec in (qw, kw):
        for pair in range(NH // 2):
            h0, h1 = 2 * pair, 2 * pair + 1
            cA = np.concatenate([h0 * HD + np.arange(64), h1 * HD + np.arange(64)])
            wn[:, i] = vec[cA]; wn[:, i + 1] = vec[cA + 64]
            i += 2
    w["wnorm_in"] = wn
    w["scale_in"] = np.ascontiguousarray(np.asarray(inp["L0_ffn_scale"]).reshape(1))
    return w


def mask_schedule():
    masks, sched, valid = [], {}, []
    for g in range(4):
        q0 = 512 * g
        for j in range(-8, 4):
            kb = q0 + 128 * j
            if kb + 128 <= 0:
                continue
            valid.append((g, j))
            kq = np.arange(128)[:, None] + kb
            qq = np.arange(512)[None, :] + q0
            m = (kq <= qq) & (qq - kq <= WINDOW - 1)
            if m.all():
                continue
            mm = m.astype(np.float32).astype(bf)
            for i, prev in enumerate(masks):
                if (prev == mm).all():
                    sched[(g, j)] = i
                    break
            else:
                masks.append(mm); sched[(g, j)] = len(masks) - 1
    return np.stack(masks), sched, valid


N_MASKS = 16
G_ORDER = [0, 1, 2, 3]
EMIT_ATTN = True
EMIT_FFN2 = True
EMIT_FFN1 = True
FFN_STAGES = "ABCDE"
SBUF_TAPS = []


class Sem:
    def __init__(self, h):
        self.h = h; self.n = 0
    def inc(self, instr, k=1):
        instr.then_inc(self.h, k); self.n += k
        return instr


def build_program(debug_taps=()):
    _, sched, valid = host_consts(N_MASKS)
    nc = bass.Bass("TRN2", target_bir_lowering=False, detect_race_conditions=False)
    ctx = ExitStack()
    with ctx:
        _emit(nc, ctx, sched, valid, set(debug_taps))
    return nc


def _emit(nc, ctx, mask_sched, mask_valid, debug_taps):
    DI = lambda n, sh, dt=F32: nc.dram_tensor(n, sh, dt, kind="ExternalInput")
    DS = lambda n, sh, dt=BF16: nc.dram_tensor(n, sh, dt)
    x_in = DI("x", [S, D])
    wqk_in = DI("wqk", [32, 128, NK, 128])
    wv_in = DI("wv", [NK, 128, D])
    wo_in = DI("wo", [NK, 128, D])
    mu_in = DI("mu", [NK, 128, ES])
    w1_in = DI("w1", [NE, NK, 128, HFF])
    w2_in = DI("w2", [NE, 4, 128, D])
    b1_in = DI("b1", [NE, HFF])
    b2_in = DI("b2", [NE, D])
    gamma_in = DI("gamma", [D])
    beta_in = DI("beta", [D])
    wnorm_in = DI("wnorm_in", [128, 32])
    scale_in = DI("scale_in", [1])
    e0_in = DI("e0", [128, 128], BF16)
    e0col_in = DI("e0col_f32", [128, 1])
    e0f_in = DI("e0f", [128, 128])
    onescol_in = DI("onescol", [128, 1], BF16)
    onesmat_in = DI("onesmat", [128, 128], BF16)
    sin_l_in = DI("sin_lhst", [128, 64])
    cos_l_in = DI("cos_lhst", [128, 64])
    pospad_in = DI("pospad", [128, S])
    ksin_in = DI("ksin", [64, S])
    kcos_in = DI("kcos", [64, S])
    masks_in = DI("masks", [N_MASKS, 128, 512], BF16)
    out = nc.dram_tensor("out", [S, D], F32, kind="ExternalOutput")

    MUN = DS("MUN", [NK, 128, ES])
    X2 = DS("X2", [S, D], F32)
    X3 = DS("X3", [S, D], F32)
    XLN = DS("XLN", [S, D])
    XN = DS("XN", [S, D])
    EXPL = DS("EXPL", [S, ES])
    SIT = DS("SIT", [NK, 128, ES])
    SOT = DS("SOT", [NE, 128, D])
    IRQ = DS("IRQ", [1, S], F32)
    HSCR = DS("HSCR", [128, HFF])
    XLN2 = DS("XLN2", [S, D])
    QT = DS("QT", [D, S])
    KT = DS("KT", [D, S])
    VT = DS("VT", [S, D])
    dbg = {}
    for name in debug_taps:
        pass
    tap_out = {}
    for name in debug_taps:
        src = {"XLN": XLN, "XN": XN, "EXPL": EXPL, "SIT": SIT, "X2": X2, "X3": X3,
               "XLN2": XLN2, "QT": QT, "KT": KT, "VT": VT, "MUN": MUN, "SOT": SOT}[name]
        shp = list(src.shape)
        tap_out[name] = (src, nc.dram_tensor("tap_" + name, shp,
                                             src.dtype if hasattr(src, "dtype") else BF16,
                                             kind="ExternalOutput"))

    SB = lambda n, sh, dt=F32: ctx.enter_context(nc.sbuf_tensor(n, sh, dt))
    e0 = SB("c_e0", [128, 128], BF16)
    e0col = SB("c_e0col", [128, 1], F32)
    e0f = SB("c_e0f", [128, 128], F32)
    onescol = SB("c_ones", [128, 1], BF16)
    onesmat = SB("c_onesm", [128, 128], BF16)
    scalepad = SB("scalepad", [128, 1], F32)
    scale_bc = SB("scale_bc", [128, 1], F32)
    gam_bc = SB("gam_bc", [128, D], BF16)
    bet_bc = SB("bet_bc", [128, D], BF16)
    wnorm = SB("wnorm", [128, 32], F32)
    cos2 = SB("cos2", [128, S], BF16)
    sin2 = SB("sin2", [128, S], BF16)
    sinl = SB("sinl", [128, 64], F32)
    cosl = SB("cosl", [128, 64], F32)
    row_a_t = SB("row_a_t", [1, ES], F32)
    row_b_t = SB("row_b_t", [1, ES], F32)
    zpad = SB("zpad", [128, ES], BF16)
    zpad_inv = SB("zpad_inv", [128, ES], F32)
    invzc_cols = SB("invzc_cols", [128, NE], F32)
    zrow_ih = SB("zrow_ih", [128, NT], F32)
    b1pad = SB("b1pad", [128, HFF], BF16)
    b2pad = SB("b2pad", [128, D], BF16)
    invrk_cols = SB("invrk_cols", [128, NT], F32)
    invrq_g = SB("invrq_g", [128, 512], F32)
    invz_bc = SB("invz_bc", [128, 512], F32)
    st_m = SB("st_m", [128, 1], F32)
    st_s = SB("st_s", [128, 1], F32)
    st_t = SB("st_t", [128, 1], F32)
    st_u = SB("st_u", [128, 1], F32)
    shiftcol = SB("shiftcol", [128, 1], F32)
    zr4 = SB("zr4", [128, 4], F32)
    qsq = SB("qsq", [128, 512], BF16)
    xt = SB("xt", [128, D], F32)
    xt2 = SB("xt2", [128, D], F32)
    lnt = SB("lnt", [128, D], BF16)
    xnt = SB("xnt", [128, D], BF16)
    big0 = SB("big0", [128, 32768], BF16)
    wbuf = SB("wbuf", [128, 4096], BF16)
    w2buf0 = SB("w2buf0", [128, D], BF16)
    w2buf1 = SB("w2buf1", [128, D], BF16)
    xg = SB("xg", [128, 8192], BF16)
    expt = SB("expt", [128, 8192], BF16)
    ebuf0 = SB("ebuf0", [128, 512], BF16)
    ebuf1 = SB("ebuf1", [128, 512], BF16)
    sobuf0 = SB("sobuf0", [128, 512], BF16)
    sobuf1 = SB("sobuf1", [128, 512], BF16)
    htile = SB("htile", [128, HFF], BF16)
    httile = SB("httile", [128, HFF], BF16)
    qtg0 = SB("qtg0", [128, 512], BF16)
    qtg1 = SB("qtg1", [128, 512], BF16)
    ktile0 = SB("ktile0", [128, 128], BF16)
    ktile1 = SB("ktile1", [128, 128], BF16)
    vtile0 = SB("vtile0", [128, 128], BF16)
    vtile1 = SB("vtile1", [128, 128], BF16)
    ot_f32 = SB("ot_f32", [128, 512], F32)

    class ColAlias:
        def __init__(self, t, off):
            self.t = t; self.off = off
        def __getitem__(self, idx):
            ps, fs = idx
            a = 0 if fs.start is None else fs.start
            bnd = fs.stop
            return self.t[ps, self.off + a: self.off + bnd]

    junk = ColAlias(xg, 0)
    invrq_row = zpad
    pairA = ColAlias(xg, 0)
    pairB = ColAlias(xg, 2048)
    roA = ColAlias(xg, 4096)
    roB = ColAlias(xg, 6144)
    otg = xg
    maskbuf = expt
    row_a = row_a_t
    row_b = row_b_t
    angbuf = ot_f32

    p0 = ctx.enter_context(nc.psum_tensor("p0", [128, 512], F32))
    p1 = ctx.enter_context(nc.psum_tensor("p1", [128, 512], F32))
    p2 = ctx.enter_context(nc.psum_tensor("p2", [128, 512], F32))
    p3 = ctx.enter_context(nc.psum_tensor("p3", [128, 512], F32))
    z0 = ctx.enter_context(nc.psum_tensor("z0", [128, 512], F32))
    z1 = ctx.enter_context(nc.psum_tensor("z1", [128, 512], F32))
    z2 = ctx.enter_context(nc.psum_tensor("z2", [128, 512], F32))
    z3 = ctx.enter_context(nc.psum_tensor("z3", [128, 512], F32))
    ZP = [z0, z1, z2, z3]
    PP = [p0, p1]
    PQ = [p2, p3]

    qL = Sem(ctx.enter_context(nc.semaphore("qL")))
    qX = Sem(ctx.enter_context(nc.semaphore("qX")))
    qG = Sem(ctx.enter_context(nc.semaphore("qG")))
    eT = Sem(ctx.enter_context(nc.semaphore("eT")))
    eV = Sem(ctx.enter_context(nc.semaphore("eV")))
    eS = Sem(ctx.enter_context(nc.semaphore("eS")))
    eG = Sem(ctx.enter_context(nc.semaphore("eG")))
    ALLSEMS = [qL, qX, qG, eT, eV, eS, eG]

    blk = ctx.enter_context(nc.Block())

    def fence():
        snap = [(s.h, s.n) for s in ALLSEMS]
        for eng in (blk.sync, blk.tensor, blk.vector, blk.scalar, blk.gpsimd):
            @eng
            def _(e, snap=snap):
                for h, n in snap:
                    if n:
                        e.wait_ge(h, n)

    def replicate_row(rhs_tile, rhs_off, dest, dest_off, W, lhs=None, rhs_f32=False):
        lhs = lhs if lhs is not None else e0
        nsl = (W + 511) // 512
        for i in range(nsl):
            w = min(512, W - i * 512)
            pb = PP[i % 2]
            @blk.tensor
            def _(t, i=i, w=w, pb=pb):
                if i >= 2:
                    t.wait_ge(eV.h, eV.n)
                eT.inc(t.matmul(pb[:, 0:w], lhs[:, :],
                                rhs_tile[0:128, rhs_off + i * 512: rhs_off + i * 512 + w],
                                start=True, stop=True))
            @blk.vector
            def _(v, i=i, w=w, pb=pb):
                v.wait_ge(eT.h, eT.n)
                eV.inc(v.tensor_copy(dest[:, dest_off + i * 512: dest_off + i * 512 + w], pb[:, 0:w]))

    @blk.sync
    def _(s):
        qL.inc(s.dma_start(out=e0[:, :], in_=e0_in.ap()), 16)
        qL.inc(s.dma_start(out=e0col[:, :], in_=e0col_in.ap()), 16)
        qL.inc(s.dma_start(out=onescol[:, :], in_=onescol_in.ap()), 16)
        qL.inc(s.dma_start(out=onesmat[:, :], in_=onesmat_in.ap()), 16)
        qL.inc(s.dma_start(out=wnorm[:, :], in_=wnorm_in.ap()), 16)
        qL.inc(s.dma_start(out=e0f[:, :], in_=e0f_in.ap()), 16)
        qL.inc(s.dma_start(out=xt[0:64, :], in_=ksin_in.ap()), 16)
        qL.inc(s.dma_start(out=xt2[0:64, :], in_=kcos_in.ap()), 16)
        qL.inc(s.dma_start(out=row_a[0:1, 0:D], in_=gamma_in.ap()[None, :]), 16)
        qL.inc(s.dma_start(out=row_b[0:1, 0:D], in_=beta_in.ap()[None, :]), 16)
        qL.inc(s.dma_start(out=sinl[:, :], in_=sin_l_in.ap()), 16)
        qL.inc(s.dma_start(out=cosl[:, :], in_=cos_l_in.ap()), 16)
        qL.inc(s.dma_start(out=zpad_inv[:, :], in_=pospad_in.ap()), 16)

    @blk.gpsimd
    def _(g):
        eG.inc(g.memset(zpad[:, :], 0.0))
        eG.inc(g.memset(b1pad[:, :], 0.0))
        eG.inc(g.memset(b2pad[:, :], 0.0))
        eG.inc(g.memset(scalepad[:, :], 0.0))
        eG.inc(g.memset(shiftcol[:, :], -SHIFT))

    @blk.sync
    def _(s):
        s.wait_ge(eG.h, eG.n)
        qL.inc(s.dma_start(out=scalepad[0:1, :], in_=scale_in.ap()[None, :]), 16)
    @blk.scalar
    def _(s):
        s.wait_ge(qL.h, qL.n)
        s.wait_ge(eG.h, eG.n)
        eS.inc(s.copy(invrq_row[0:1, 0:D], row_a[0:1, 0:D]))
    @blk.tensor
    def _(t):
        t.wait_ge(eS.h, eS.n)
    replicate_row(invrq_row, 0, gam_bc, 0, D)
    @blk.scalar
    def _(s):
        s.wait_ge(eV.h, eV.n)
        eS.inc(s.copy(invrq_row[0:1, 0:D], row_b[0:1, 0:D]))
    @blk.tensor
    def _(t):
        t.wait_ge(eS.h, eS.n)
    replicate_row(invrq_row, 0, bet_bc, 0, D)
    @blk.tensor
    def _(t):
        t.wait_ge(qL.h, qL.n)
        eT.inc(t.matmul(p2[:, 0:1], e0f[:, :], scalepad[0:128, 0:1], start=True, stop=True))
    @blk.vector
    def _(v):
        v.wait_ge(eT.h, eT.n)
        eV.inc(v.tensor_copy(scale_bc[:, :], p2[:, 0:1]))
    fence()

    for trig, lhsT_t, ktab in (("sin", sinl, xt), ("cos", cosl, xt2)):
        for q4 in range(4):
            sl = slice(q4 * 512, q4 * 512 + 512)
            @blk.tensor
            def _(t, lhsT_t=lhsT_t, sl=sl):
                t.wait_ge(eV.h, eV.n)
                eT.inc(t.matmul(p2[0:64, :], lhsT_t[:, 0:64], zpad_inv[0:128, sl],
                                start=True, stop=True))
            @blk.vector
            def _(v, ktab=ktab, sl=sl):
                v.wait_ge(eT.h, eT.n)
                v.wait_ge(eS.h, eS.n)
                v.tensor_copy(angbuf[0:64, :], p2[0:64, :])
                v.scalar_tensor_tensor(p3[0:64, :], ktab[0:64, sl], -CW1, angbuf[0:64, :], ALU.mult, ALU.add)
                v.scalar_tensor_tensor(angbuf[0:64, :], ktab[0:64, sl], -CW2, p3[0:64, :], ALU.mult, ALU.add)
                eV.inc(v.scalar_tensor_tensor(p3[0:64, :], ktab[0:64, sl], -CW3, angbuf[0:64, :], ALU.mult, ALU.add))
            @blk.scalar
            def _(s, trig=trig, sl=sl):
                s.wait_ge(eV.h, eV.n)
                dst = sin2 if trig == "sin" else cos2
                s.activation(dst[0:64, sl], p3[0:64, :], AF.Sin)
                eS.inc(s.copy(dst[64:128, sl], dst[0:64, sl]))
    @blk.gpsimd
    def _(g):
        g.wait_ge(eT.h, eT.n)
        eG.inc(g.memset(zpad_inv[:, :], 0.0))
    fence()

    XTD = [xt, xt2]
    for k in range(NK):
        xk = XTD[k % 2]
        @blk.sync
        def _(s, k=k, xk=xk):
            if k >= 2:
                s.wait_ge(eS.h, eS.n)
            qL.inc(s.dma_start(out=xk[:, :], in_=mu_in.ap()[k]), 16)
        @blk.scalar
        def _(s, xk=xk):
            s.wait_ge(qL.h, qL.n)
            eS.inc(s.activation(junk[:, 0:D], xk[:, :], AF.Square))
        @blk.tensor
        def _(t, k=k):
            t.wait_ge(eS.h, eS.n)
            for sn in range(4):
                mm = t.matmul(ZP[sn][0:1, :], onescol[:, :], junk[:, sn * 512:(sn + 1) * 512],
                              start=(k == 0), stop=(k == NK - 1))
                if k == NK - 1:
                    eT.inc(mm)
    @blk.vector
    def _(v):
        v.wait_ge(eT.h, eT.n)
        for sn in range(4):
            v.tensor_copy(row_a[0:1, sn * 512:(sn + 1) * 512], ZP[sn][0:1, :])
        eV.inc(v.tensor_copy(row_b[0:1, 0:1], row_a[0:1, 0:1]))
    @blk.scalar
    def _(s):
        s.wait_ge(eV.h, eV.n)
        eS.inc(s.activation(row_b[0:1, :], row_a[0:1, :], AF.Sqrt))
    @blk.vector
    def _(v):
        v.wait_ge(eS.h, eS.n)
        v.tensor_scalar(row_a[0:1, :], row_b[0:1, :], L2_EPS, None, ALU.max)
        v.drain()
        eV.inc(v.reciprocal(row_b[0:1, :], row_a[0:1, :]))
    @blk.scalar
    def _(s):
        s.wait_ge(eV.h, eV.n)
        eS.inc(s.copy(invrq_row[0:1, 0:ES], row_b[0:1, 0:ES]))
    @blk.tensor
    def _(t):
        t.wait_ge(eS.h, eS.n)
    replicate_row(invrq_row, 0, big0, 0, ES)
    fence()
    MSET = [(ColAlias(xg, 0), ColAlias(xg, 2048)), (ColAlias(xg, 4096), ColAlias(xg, 6144))]
    for k in range(NK):
        a, bb = MSET[k % 2]
        @blk.gpsimd
        def _(g, k=k, a=a, bb=bb):
            if k >= 2:
                g.wait_ge(qG.h, qG.n)
            qG.inc(g.dma_start(out=a[:, 0:ES], in_=mu_in.ap()[k]), 16)
            g.wait_ge(qG.h, qG.n)
            eG.inc(g.tensor_tensor(bb[:, 0:ES], a[:, 0:ES], big0[:, 0:ES], ALU.mult))
            qG.inc(g.dma_start(out=MUN.ap()[k], in_=bb[:, 0:ES]), 16)
    fence()

    EB = [ebuf0, ebuf1]
    SOB = [sobuf0, sobuf1]
    PPALL = [p0, p1, p2, p3]
    def ffn_call(tag, xsrc, xdst, stages="ABCDE"):
        inv2048 = float(1.0 / 2048.0)
        for m in range(NT):
            xa = XTD[m % 2]
            ldq = None
            @blk.sync
            def _(s, m=m, xa=xa):
                s.wait_ge(eS.h, eS.n)
                s.wait_ge(eV.h, eV.n)
                qL.inc(s.dma_start(out=xa[:, :], in_=xsrc(m)), 16)
            @blk.vector
            def _(v, xa=xa):
                v.wait_ge(qL.h, qL.n)
                v.wait_ge(qG.h, qG.n)
                v.wait_ge(eS.h, eS.n)
                v.tensor_reduce(st_s[:, :], xa[:, :], AX.X, ALU.add)
                v.drain()
                eV.inc(v.tensor_scalar(st_m[:, :], st_s[:, :], inv2048, None, ALU.mult))
            @blk.scalar
            def _(s, xa=xa):
                s.wait_ge(eV.h, eV.n)
                s.activation(junk[:, 0:D], xa[:, :], AF.Square, accum_out=st_s[:, :])
                eS.inc(s.activation(st_t[:, :], st_s[:, :], AF.Copy, scale=inv2048))
            @blk.vector
            def _(v):
                v.wait_ge(eS.h, eS.n)
                v.tensor_scalar(st_u[:, :], st_m[:, :], st_m[:, 0:1], st_t[:, 0:1],
                                ALU.mult, ALU.subtract)
                v.drain()
                eV.inc(v.tensor_scalar(st_t[:, :], st_u[:, :], -1.0, LN_EPS, ALU.mult, ALU.add))
            @blk.scalar
            def _(s):
                s.wait_ge(eV.h, eV.n)
                eS.inc(s.activation(st_u[:, :], st_t[:, :], AF.Sqrt))
            @blk.vector
            def _(v):
                v.wait_ge(eS.h, eS.n)
                v.reciprocal(st_t[:, :], st_u[:, :])
                v.drain()
                v.tensor_scalar(st_u[:, :], st_t[:, :], st_t[:, 0:1], -LN_EPS, ALU.mult, ALU.mult)
                v.drain()
                eV.inc(v.tensor_scalar(st_s[:, :], st_u[:, :], 1.0 + LN_EPS, None, ALU.add))
            @blk.scalar
            def _(s):
                s.wait_ge(eV.h, eV.n)
                eS.inc(s.activation(st_u[:, :], st_s[:, :], AF.Sqrt))
            @blk.vector
            def _(v, xa=xa):
                v.wait_ge(eS.h, eS.n)
                v.reciprocal(st_s[:, :], st_u[:, :])
                v.drain()
                v.tensor_scalar(st_u[:, :], st_t[:, :], st_s[:, 0:1], None, ALU.mult)
                v.drain()
                eV.inc(v.tensor_scalar(lnt[:, :], xa[:, :], st_m[:, 0:1], st_u[:, 0:1],
                                       ALU.subtract, ALU.mult))
            @blk.gpsimd
            def _(g):
                g.wait_ge(eV.h, eV.n)
                g.tensor_tensor(xnt[:, :], lnt[:, :], gam_bc[:, :], ALU.mult)
                eG.inc(g.tensor_tensor(lnt[:, :], xnt[:, :], bet_bc[:, :], ALU.add))
            @blk.scalar
            def _(s):
                s.wait_ge(eG.h, eG.n)
                s.activation(junk[:, 0:D], lnt[:, :], AF.Square, accum_out=st_s[:, :])
                eS.inc(s.activation(st_t[:, :], st_s[:, :], AF.Sqrt))
            @blk.vector
            def _(v):
                v.wait_ge(eS.h, eS.n)
                v.tensor_scalar(st_u[:, :], st_t[:, :], L2_EPS, None, ALU.max)
                v.drain()
                v.reciprocal(st_t[:, :], st_u[:, :])
                v.drain()
                v.tensor_scalar(st_u[:, :], st_t[:, :], scale_bc[:, 0:1], None, ALU.mult)
                v.drain()
                eV.inc(v.tensor_scalar(xnt[:, :], lnt[:, :], st_u[:, 0:1], None, ALU.mult))
            @blk.gpsimd
            def _(g, m=m):
                g.wait_ge(eV.h, eV.n)
                qG.inc(g.dma_start(out=XLN.ap()[m * 128:(m + 1) * 128, :], in_=lnt[:, :]), 16)
                qG.inc(g.dma_start(out=XN.ap()[m * 128:(m + 1) * 128, :], in_=xnt[:, :]), 16)
        fence()

        if "B" not in stages:
            return
        @blk.sync
        def _(s):
            for k in range(NK):
                qL.inc(s.dma_start(out=big0[:, k * ES:(k + 1) * ES], in_=MUN.ap()[k]), 16)
        ecount = [0]
        for grp in range(4):
            @blk.scalar
            def _(s, grp=grp):
                s.wait_ge(qG.h, qG.n)
                if grp >= 1:
                    s.wait_ge(eT.h, eT.n)
                for k in range(NK):
                    qX.inc(s.dma_start_transpose(
                        out=xg[:, k * 512:(k + 1) * 512],
                        in_=XN.ap()[grp * 512:(grp + 1) * 512, k * 128:(k + 1) * 128]), 16)
            for mi in range(4):
                m = grp * 4 + mi
                for sn in range(4):
                    pb = PP[(m * 4 + sn) % 2]
                    eb = EB[ecount[0] % 2]
                    ecount[0] += 1
                    @blk.tensor
                    def _(t, grp=grp, mi=mi, sn=sn, pb=pb, m=m):
                        t.wait_ge(qL.h, qL.n)
                        t.wait_ge(qX.h, qX.n)
                        t.wait_ge(eS.h, eS.n)
                        for k in range(NK):
                            mm = t.matmul(pb[:, :],
                                          xg[:, k * 512 + mi * 128: k * 512 + mi * 128 + 128],
                                          big0[:, k * ES + sn * 512: k * ES + sn * 512 + 512],
                                          start=(k == 0), stop=(k == NK - 1))
                        eT.inc(mm)
                    @blk.scalar
                    def _(s, pb=pb, eb=eb, sn=sn):
                        s.wait_ge(eT.h, eT.n)
                        s.wait_ge(qG.h, qG.n)
                        s.wait_ge(eT.h, eT.n)
                        s.activation(eb[:, :], pb[:, :], AF.Exp,
                                     accum_out=zr4[:, sn:sn + 1])
                        eS.inc(s.copy(st_u[:, 0:1], zr4[:, sn:sn + 1]))
                    @blk.tensor
                    def _(t, eb=eb, sn=sn, m=m):
                        t.wait_ge(eS.h, eS.n)
                        mm = t.matmul(ZP[sn][0:1, :], onescol[:, :], eb[:, :],
                                      start=(m == 0), stop=(m == NT - 1))
                        if m == NT - 1:
                            eT.inc(mm)
                    @blk.gpsimd
                    def _(g, eb=eb, m=m, sn=sn):
                        g.wait_ge(eS.h, eS.n)
                        qG.inc(g.dma_start(out=EXPL.ap()[m * 128:(m + 1) * 128,
                                                         sn * 512:(sn + 1) * 512],
                                           in_=eb[:, :]), 16)
                @blk.vector
                def _(v, m=m):
                    v.wait_ge(eS.h, eS.n)
                    v.tensor_reduce(st_s[:, :], zr4[:, :], AX.X, ALU.add)
                    v.drain()
                    v.reciprocal(st_t[:, :], st_s[:, :])
                    v.drain()
                    eV.inc(v.tensor_scalar(zrow_ih[:, m:m + 1], st_t[:, :], 0.5, None, ALU.mult))
        @blk.vector
        def _(v):
            v.wait_ge(eT.h, eT.n)
            for sn in range(4):
                v.tensor_copy(row_a[0:1, sn * 512:(sn + 1) * 512], ZP[sn][0:1, :])
            v.drain()
            eV.inc(v.reciprocal(row_b[0:1, :], row_a[0:1, :]))
        @blk.scalar
        def _(s):
            s.wait_ge(eV.h, eV.n)
            s.copy(zpad[0:1, 0:ES], row_a[0:1, 0:ES])
            eS.inc(s.copy(zpad_inv[0:1, 0:ES], row_b[0:1, 0:ES]))
        @blk.tensor
        def _(t):
            t.wait_ge(eS.h, eS.n)
            for e in range(NE):
                mm = t.matmul(p2[:, e:e + 1], zpad_inv[:, e * 128:(e + 1) * 128],
                              e0col[:, :], start=True, stop=True)
            eT.inc(mm)
        @blk.vector
        def _(v):
            v.wait_ge(eT.h, eT.n)
            eV.inc(v.tensor_copy(invzc_cols[:, :], p2[:, 0:NE]))
        fence()

        if "C" not in stages:
            return
        @blk.sync
        def _(s):
            for m in range(NT):
                qL.inc(s.dma_start(out=big0[:, m * D:(m + 1) * D],
                                   in_=XLN.ap()[m * 128:(m + 1) * 128, :]), 16)
        for sn in range(4):
            for quad in range(4):
                dbs = [quad * 4 + i for i in range(4)]
                snapC = {}
                for m in range(NT):
                    eb = EB[m % 2]
                    @blk.sync
                    def _(s, m=m, sn=sn, eb=eb):
                        if m >= 2:
                            s.wait_ge(eT.h, snapC[m - 2])
                        qL.inc(s.dma_start(out=eb[:, :],
                                           in_=EXPL.ap()[m * 128:(m + 1) * 128,
                                                         sn * 512:(sn + 1) * 512]), 16)
                    @blk.tensor
                    def _(t, m=m, sn=sn, eb=eb, dbs=dbs):
                        t.wait_ge(qL.h, qL.n)
                        if m == 0:
                            t.wait_ge(eV.h, eV.n)
                        for i, db in enumerate(dbs):
                            mm = t.matmul(PPALL[i][:, :],
                                          big0[:, m * D + db * 128: m * D + db * 128 + 128],
                                          eb[:, :],
                                          start=(m == 0), stop=(m == NT - 1))
                        eT.inc(mm)
                    snapC[m] = eT.n
                @blk.vector
                def _(v, sn=sn, dbs=dbs):
                    v.wait_ge(eT.h, eT.n)
                    for i, db in enumerate(dbs):
                        eV.inc(v.tensor_copy(xg[:, i * 512:(i + 1) * 512], PPALL[i][:, :]))
                @blk.gpsimd
                def _(g, sn=sn, dbs=dbs):
                    g.wait_ge(eV.h, eV.n)
                    for i, db in enumerate(dbs):
                        qG.inc(g.dma_start(out=SIT.ap()[db][:, sn * 512:(sn + 1) * 512],
                                           in_=xg[:, i * 512:(i + 1) * 512]), 16)
        fence()

        if "D" not in stages:
            return
        W2B = [w2buf0, w2buf1]
        for e in range(NE):
            @blk.gpsimd
            def _(g, e=e):
                g.wait_ge(eT.h, eT.n)
                qG.inc(g.dma_start(out=wbuf[:, 0:2048].rearrange("p (k f) -> p k f", k=4),
                                   in_=w1_in.ap()[e, 0:4, :, :].rearrange("k p f -> p k f")), 16)
                qG.inc(g.dma_start(out=b1pad[0:1, :], in_=b1_in.ap()[e][None, :]), 16)
                qG.inc(g.dma_start(out=b2pad[0:1, :], in_=b2_in.ap()[e][None, :]), 16)
            @blk.sync
            def _(s, e=e):
                s.wait_ge(eT.h, eT.n)
                for k in range(NK):
                    qL.inc(s.dma_start(out=expt[:, k * 128:(k + 1) * 128],
                                       in_=SIT.ap()[k][:, e * 128:(e + 1) * 128]), 16)
            snapD = {}
            for qi in range(4):
                @blk.gpsimd
                def _(g, e=e, qi=qi):
                    if qi == 0:
                        return
                    if qi >= 2:
                        g.wait_ge(eT.h, snapD[qi - 2])
                    qG.inc(g.dma_start(
                        out=wbuf[:, (qi % 2) * 2048:(qi % 2) * 2048 + 2048].rearrange("p (k f) -> p k f", k=4),
                        in_=w1_in.ap()[e, 4 * qi:4 * qi + 4, :, :].rearrange("k p f -> p k f")), 16)
                @blk.tensor
                def _(t, e=e, qi=qi):
                    t.wait_ge(qG.h, qG.n)
                    if qi == 0:
                        t.wait_ge(qL.h, qL.n)
                        t.wait_ge(eS.h, eS.n)
                        t.matmul(p0[:, :], zpad[:, e * 128:(e + 1) * 128], b1pad[:, :],
                                 start=True, stop=False)
                    for kk in range(4):
                        k = 4 * qi + kk
                        mm = t.matmul(p0[:, :], expt[:, k * 128:(k + 1) * 128],
                                      wbuf[:, (qi % 2) * 2048 + kk * 512:(qi % 2) * 2048 + kk * 512 + 512],
                                      start=False, stop=(k == NK - 1))
                    eT.inc(mm)
                snapD[qi] = eT.n
            @blk.scalar
            def _(s, e=e):
                s.wait_ge(eT.h, eT.n)
                s.wait_ge(qX.h, qX.n)
                eS.inc(s.activation(htile[:, :], p0[:, :], AF.Gelu,
                                    scale=invzc_cols[:, e:e + 1]))
            @blk.gpsimd
            def _(g):
                g.wait_ge(eS.h, eS.n)
                g.wait_ge(qX.h, qX.n)
                qG.inc(g.dma_start(out=HSCR.ap()[:, :], in_=htile[:, :]), 16)
            @blk.scalar
            def _(s):
                s.wait_ge(qG.h, qG.n)
                for hc in range(4):
                    qX.inc(s.dma_start_transpose(
                        out=httile[:, hc * 128:(hc + 1) * 128],
                        in_=HSCR.ap()[0:128, hc * 128:(hc + 1) * 128]), 16)
            for dn in range(4):
                wb = W2B[dn % 2]
                @blk.gpsimd
                def _(g, e=e, dn=dn, wb=wb):
                    if dn >= 2 or e > 0:
                        g.wait_ge(eT.h, eT.n)
                    qG.inc(g.dma_start(
                        out=wb[:, :].rearrange("p (h f) -> p h f", h=4),
                        in_=w2_in.ap()[e, :, :, dn * 512:(dn + 1) * 512]
                        .rearrange("h p f -> p h f")), 16)
                @blk.tensor
                def _(t, e=e, dn=dn, wb=wb):
                    t.wait_ge(qG.h, qG.n)
                    t.wait_ge(qX.h, qX.n)
                    if dn == 0:
                        t.wait_ge(eV.h, eV.n)
                    t.matmul(p1[:, :], e0[:, :], b2pad[:, dn * 512:(dn + 1) * 512],
                             start=True, stop=False)
                    for hc in range(4):
                        mm = t.matmul(p1[:, :], httile[:, hc * 128:(hc + 1) * 128],
                                      wb[:, hc * 512:(hc + 1) * 512],
                                      start=False, stop=(hc == 3))
                    eT.inc(mm)
                @blk.vector
                def _(v, e=e, dn=dn):
                    v.wait_ge(eT.h, eT.n)
                    v.wait_ge(qG.h, qG.n)
                    eV.inc(v.tensor_copy(qsq[:, :], p1[:, :]))
                @blk.gpsimd
                def _(g, e=e, dn=dn):
                    g.wait_ge(eV.h, eV.n)
                    qG.inc(g.dma_start(out=SOT.ap()[e][:, dn * 512:(dn + 1) * 512],
                                       in_=qsq[:, :]), 16)
        fence()

        if "E" not in stages:
            return
        @blk.sync
        def _(s):
            s.wait_ge(qG.h, qG.n)
            for m in range(NT):
                qL.inc(s.dma_start(out=xdst(m), in_=xsrc(m)), 16)
        for grp in range(4):
            @blk.scalar
            def _(s, grp=grp):
                s.wait_ge(eT.h, eT.n)
                for sc in range(NK):
                    qX.inc(s.dma_start_transpose(
                        out=expt[:, sc * 512:(sc + 1) * 512],
                        in_=EXPL.ap()[grp * 512:(grp + 1) * 512, sc * 128:(sc + 1) * 128]), 16)
            for dn in range(4):
                snapE = {}
                for sc in range(NK):
                    sob = SOB[sc % 2]
                    @blk.sync
                    def _(s, grp=grp, dn=dn, sc=sc, sob=sob):
                        s.wait_ge(qL.h, qL.n)
                        if sc >= 2:
                            s.wait_ge(eT.h, snapE[sc - 2])
                        qL.inc(s.dma_start(out=sob[:, :],
                                           in_=SOT.ap()[sc][:, dn * 512:(dn + 1) * 512]), 16)
                    @blk.tensor
                    def _(t, grp=grp, dn=dn, sc=sc, sob=sob):
                        t.wait_ge(qL.h, qL.n)
                        t.wait_ge(qX.h, qX.n)
                        if sc == 0:
                            t.wait_ge(eV.h, eV.n)
                        for mi in range(4):
                            mm = t.matmul(PPALL[mi][:, :],
                                          expt[:, sc * 512 + mi * 128: sc * 512 + mi * 128 + 128],
                                          sob[:, :],
                                          start=(sc == 0), stop=(sc == NK - 1))
                        eT.inc(mm)
                    snapE[sc] = eT.n
                for mi in range(4):
                    m = grp * 4 + mi
                    @blk.vector
                    def _(v, m=m, mi=mi):
                        v.wait_ge(eT.h, eT.n)
                        v.wait_ge(qG.h, qG.n)
                        eV.inc(v.tensor_scalar(ot_f32[:, :], PPALL[mi][:, :],
                                               zrow_ih[:, m:m + 1], None, ALU.mult))
                    @blk.gpsimd
                    def _(g, m=m, dn=dn):
                        g.wait_ge(eV.h, eV.n)
                        qG.inc(g.dma_start(out=xdst(m, dn * 512, (dn + 1) * 512),
                                           in_=ot_f32[:, :], accum_op=ALU.add), 16)
        fence()

    def _sl(t):
        def f(m, c0=0, c1=D):
            return t.ap()[m * 128:(m + 1) * 128, c0:c1]
        return f
    xsrc_input = _sl(x_in)
    xsrc_X2 = _sl(X2); xdst_X2 = _sl(X2)
    xsrc_X3 = _sl(X3); xdst_X3 = _sl(X3)
    xdst_out = _sl(out)

    if EMIT_FFN1:
        ffn_call("ffn1", xsrc_input, xdst_X2, stages=FFN_STAGES)

    if EMIT_ATTN:
        inv2048 = float(1.0 / 2048.0)
        for m in range(NT):
            xa = XTD[m % 2]
            @blk.sync
            def _(s, m=m, xa=xa):
                s.wait_ge(eS.h, eS.n)
                s.wait_ge(eV.h, eV.n)
                qL.inc(s.dma_start(out=xa[:, :], in_=xsrc_X2(m)), 16)
            @blk.vector
            def _(v, xa=xa):
                v.wait_ge(qL.h, qL.n)
                v.wait_ge(qG.h, qG.n)
                v.wait_ge(eS.h, eS.n)
                v.tensor_reduce(st_s[:, :], xa[:, :], AX.X, ALU.add)
                v.drain()
                eV.inc(v.tensor_scalar(st_m[:, :], st_s[:, :], inv2048, None, ALU.mult))
            @blk.scalar
            def _(s, xa=xa):
                s.wait_ge(eV.h, eV.n)
                s.activation(junk[:, 0:D], xa[:, :], AF.Square, accum_out=st_s[:, :])
                eS.inc(s.activation(st_t[:, :], st_s[:, :], AF.Copy, scale=inv2048))
            @blk.vector
            def _(v):
                v.wait_ge(eS.h, eS.n)
                v.tensor_scalar(st_u[:, :], st_m[:, :], st_m[:, 0:1], st_t[:, 0:1],
                                ALU.mult, ALU.subtract)
                v.drain()
                eV.inc(v.tensor_scalar(st_t[:, :], st_u[:, :], -1.0, LN_EPS, ALU.mult, ALU.add))
            @blk.scalar
            def _(s):
                s.wait_ge(eV.h, eV.n)
                eS.inc(s.activation(st_u[:, :], st_t[:, :], AF.Sqrt))
            @blk.vector
            def _(v, xa=xa):
                v.wait_ge(eS.h, eS.n)
                v.reciprocal(st_t[:, :], st_u[:, :])
                v.drain()
                eV.inc(v.tensor_scalar(lnt[:, :], xa[:, :], st_m[:, 0:1], st_t[:, 0:1],
                                       ALU.subtract, ALU.mult))
            @blk.gpsimd
            def _(g, m=m):
                g.wait_ge(eV.h, eV.n)
                qG.inc(g.dma_start(out=XLN2.ap()[m * 128:(m + 1) * 128, :], in_=lnt[:, :]), 16)
            fence()
        fence()
        @blk.scalar
        def _(s):
            for k in range(NK):
                for tn in range(4):
                    qX.inc(s.dma_start_transpose(
                        out=big0[:, k * S + tn * 512: k * S + tn * 512 + 512],
                        in_=XLN2.ap()[tn * 512:(tn + 1) * 512, k * 128:(k + 1) * 128]), 16)
        fence()

        for fb in range(32):
            half = fb % 2
            isq = fb < 16
            first_of_kind = (fb % 16) == 0
            last_of_kind = (fb % 16) == 15
            @blk.gpsimd
            def _(g, fb=fb, half=half):
                g.wait_ge(eT.h, eT.n)
                qG.inc(g.dma_start(out=wbuf[:, half * 2048:half * 2048 + 2048],
                                   in_=wqk_in.ap()[fb].rearrange("p k f -> p (k f)")), 16)
            ptile = pairA if fb % 2 == 0 else pairB
            for tn in range(4):
                pb = PP[tn % 2]
                @blk.tensor
                def _(t, fb=fb, half=half, tn=tn, pb=pb):
                    t.wait_ge(qG.h, qG.n)
                    t.wait_ge(qX.h, qX.n)
                    t.wait_ge(eS.h, eS.n)
                    t.wait_ge(eV.h, eV.n)
                    for k in range(NK):
                        mm = t.matmul(pb[:, :],
                                      wbuf[:, half * 2048 + k * 128: half * 2048 + k * 128 + 128],
                                      big0[:, k * S + tn * 512: k * S + tn * 512 + 512],
                                      start=(k == 0), stop=(k == NK - 1))
                    eT.inc(mm)
                @blk.scalar
                def _(s, fb=fb, tn=tn, pb=pb, ptile=ptile):
                    s.wait_ge(eT.h, eT.n)
                    s.wait_ge(eV.h, eV.n)
                    s.activation(qsq[:, :], pb[:, :], AF.Square)
                    eS.inc(s.activation(ptile[:, tn * 512:(tn + 1) * 512], pb[:, :],
                                        AF.Copy, scale=wnorm[:, fb:fb + 1]))
                @blk.tensor
                def _(t, fb=fb, tn=tn, first_of_kind=first_of_kind, last_of_kind=last_of_kind):
                    t.wait_ge(eS.h, eS.n)
                    mm = t.matmul(ZP[tn][0:128, :], onesmat[:, :], qsq[:, :],
                                  start=first_of_kind, stop=last_of_kind)
                    if last_of_kind:
                        eT.inc(mm)
            if fb % 2 == 1:
                @blk.vector
                def _(v):
                    v.wait_ge(eS.h, eS.n)
                    v.wait_ge(qG.h, qG.n)
                    eV.inc(v.tensor_tensor(xt[:, :], pairA[:, 0:S], cos2[:, :], ALU.mult))
                @blk.gpsimd
                def _(g):
                    g.wait_ge(eS.h, eS.n)
                    g.wait_ge(qG.h, qG.n)
                    eG.inc(g.tensor_tensor(xt2[:, :], pairB[:, 0:S], sin2[:, :], ALU.mult))
                @blk.vector
                def _(v):
                    v.wait_ge(eG.h, eG.n)
                    eV.inc(v.tensor_tensor(roA[:, 0:S], xt[:, :], xt2[:, :], ALU.subtract))
                    eV.inc(v.tensor_tensor(xt[:, :], pairB[:, 0:S], cos2[:, :], ALU.mult))
                @blk.gpsimd
                def _(g):
                    g.wait_ge(eV.h, eV.n)
                    eG.inc(g.tensor_tensor(xt2[:, :], pairA[:, 0:S], sin2[:, :], ALU.mult))
                @blk.vector
                def _(v):
                    v.wait_ge(eG.h, eG.n)
                    eV.inc(v.tensor_tensor(roB[:, 0:S], xt[:, :], xt2[:, :], ALU.add))
                fence()
                @blk.gpsimd
                def _(g, fb=fb, isq=isq):
                    g.wait_ge(eV.h, eV.n)
                    dstT = QT if isq else KT
                    pr = (fb % 16) // 2
                    h0 = 2 * pr; h1 = h0 + 1
                    qG.inc(g.dma_start(out=dstT.ap()[h0 * HD + 0: h0 * HD + 64, :],
                                       in_=roA[0:64, 0:S]), 16)
                    qG.inc(g.dma_start(out=dstT.ap()[h1 * HD + 0: h1 * HD + 64, :],
                                       in_=roA[64:128, 0:S]), 16)
                    qG.inc(g.dma_start(out=dstT.ap()[h0 * HD + 64: h0 * HD + 128, :],
                                       in_=roB[0:64, 0:S]), 16)
                    qG.inc(g.dma_start(out=dstT.ap()[h1 * HD + 64: h1 * HD + 128, :],
                                       in_=roB[64:128, 0:S]), 16)
            if last_of_kind:
                @blk.vector
                def _(v):
                    v.wait_ge(eT.h, eT.n)
                    for tn in range(4):
                        v.tensor_copy(row_a[0:1, tn * 512:(tn + 1) * 512], ZP[tn][0:1, :])
                    v.drain()
                    eV.inc(v.tensor_scalar(row_b[0:1, :], row_a[0:1, :], inv2048, RMS_EPS,
                                           ALU.mult, ALU.add))
                @blk.scalar
                def _(s):
                    s.wait_ge(eV.h, eV.n)
                    eS.inc(s.activation(row_a[0:1, :], row_b[0:1, :], AF.Sqrt))
                @blk.vector
                def _(v):
                    v.wait_ge(eS.h, eS.n)
                    eV.inc(v.reciprocal(row_b[0:1, :], row_a[0:1, :]))
                if isq:
                    @blk.scalar
                    def _(s):
                        s.wait_ge(eV.h, eV.n)
                        eS.inc(s.activation(row_a[0:1, :], row_b[0:1, :], AF.Copy, scale=SC_HI))
                    @blk.vector
                    def _(v):
                        v.wait_ge(eS.h, eS.n)
                        eV.inc(v.scalar_tensor_tensor(zpad_inv[0:1, 0:S], row_b[0:1, :], SC_LO,
                                                      row_a[0:1, :], ALU.mult, ALU.add))
                    @blk.gpsimd
                    def _(g):
                        g.wait_ge(eV.h, eV.n)
                        qG.inc(g.dma_start(out=IRQ.ap()[0:1, :], in_=zpad_inv[0:1, 0:S]), 16)
                else:
                    @blk.scalar
                    def _(s):
                        s.wait_ge(eV.h, eV.n)
                        s.wait_ge(qG.h, qG.n)
                        eS.inc(s.copy(zpad_inv[0:1, 0:S], row_b[0:1, 0:S]))
                    @blk.tensor
                    def _(t):
                        t.wait_ge(eS.h, eS.n)
                        for kb in range(NT):
                            mm = t.matmul(p2[:, kb:kb + 1], zpad_inv[:, kb * 128:(kb + 1) * 128],
                                          e0col[:, :], start=True, stop=True)
                        eT.inc(mm)
                    @blk.vector
                    def _(v):
                        v.wait_ge(eT.h, eT.n)
                        eV.inc(v.tensor_copy(invrk_cols[:, :], p2[:, 0:NT]))
        fence()

        for fn8 in range(8):
            @blk.gpsimd
            def _(g, fn8=fn8):
                g.wait_ge(eT.h, eT.n)
                qG.inc(g.dma_start(out=wbuf[:, 0:4096].rearrange("p (k f) -> p k f", k=NK),
                                   in_=wv_in.ap()[:, :, fn8 * 256:(fn8 + 1) * 256]
                                   .rearrange("k p f -> p k f")), 16)
            for m in range(NT):
                pb = PP[m % 2]
                eb = EB[m % 2]
                @blk.tensor
                def _(t, fn8=fn8, m=m, pb=pb):
                    t.wait_ge(qG.h, qG.n)
                    t.wait_ge(eV.h, eV.n)
                    for k in range(NK):
                        mm = t.matmul(pb[:, 0:256],
                                      big0[:, k * S + m * 128: k * S + m * 128 + 128],
                                      wbuf[:, k * 256:(k + 1) * 256],
                                      start=(k == 0), stop=(k == NK - 1))
                    eT.inc(mm)
                @blk.vector
                def _(v, pb=pb, eb=eb):
                    v.wait_ge(eT.h, eT.n)
                    v.wait_ge(qG.h, qG.n)
                    eV.inc(v.tensor_copy(eb[:, 0:256], pb[:, 0:256]))
                @blk.gpsimd
                def _(g, fn8=fn8, m=m, eb=eb):
                    g.wait_ge(eV.h, eV.n)
                    qG.inc(g.dma_start(out=VT.ap()[m * 128:(m + 1) * 128,
                                                   fn8 * 256:(fn8 + 1) * 256],
                                       in_=eb[:, 0:256]), 16)
            fence()
        fence()

        @blk.sync
        def _(s):
            for m in range(NT):
                qL.inc(s.dma_start(out=xdst_X3(m), in_=xsrc_X2(m)), 16)
        KTIL = [ktile0, ktile1]
        VTIL = [vtile0, vtile1]
        MEB = [qsq, htile]
        for g_ in G_ORDER:
            vblocks = [j for (gg, j) in mask_valid if gg == g_]
            mask_slots = {}
            @blk.sync
            def _(s, g_=g_, vblocks=vblocks):
                s.wait_ge(eV.h, eV.n)
                for i, j in enumerate(vblocks):
                    idx = mask_sched.get((g_, j), N_MASKS - 1)
                    mask_slots[j] = i
                    qL.inc(s.dma_start(out=expt[:, i * 512:(i + 1) * 512],
                                       in_=masks_in.ap()[idx]), 16)
            @blk.sync
            def _(s, g_=g_):
                s.wait_ge(eT.h, eT.n)
                qL.inc(s.dma_start(out=zpad_inv[0:1, 512:1024],
                                   in_=IRQ.ap()[0:1, g_ * 512:(g_ + 1) * 512]), 16)
            @blk.tensor
            def _(t):
                t.wait_ge(qL.h, qL.n)
                t.wait_ge(eV.h, eV.n)
                eT.inc(t.matmul(p2[:, :], e0f[:, :], zpad_inv[0:128, 512:1024],
                                start=True, stop=True))
            @blk.vector
            def _(v):
                v.wait_ge(eT.h, eT.n)
                eV.inc(v.tensor_copy(invrq_g[:, :], p2[:, :]))
            for h in range(NH):
                pq = PQ[h % 2]
                @blk.sync
                def _(s, g_=g_, h=h):
                    s.wait_ge(eV.h, eV.n)
                    qL.inc(s.dma_start(out=qtg0[:, :],
                                       in_=QT.ap()[h * HD:(h + 1) * HD,
                                                   g_ * 512:(g_ + 1) * 512]), 16)
                @blk.vector
                def _(v, g_=g_):
                    v.wait_ge(qL.h, qL.n)
                    v.wait_ge(eT.h, eT.n)
                    eV.inc(v.tensor_tensor(qtg1[:, :], qtg0[:, :],
                                           invrq_g[:, :], ALU.mult))
                fence()
                for idx, j in enumerate(vblocks):
                    kb = g_ * 4 + j
                    pb = PP[idx % 2]
                    kt = KTIL[idx % 2]
                    vt = VTIL[idx % 2]
                    meb = MEB[idx % 2]
                    eb = EB[idx % 2]
                    masked = (g_, j) in mask_sched
                    @blk.sync
                    def _(s, h=h, kb=kb, kt=kt, vt=vt):
                        if True:
                            s.wait_ge(eT.h, eT.n)
                        qL.inc(s.dma_start(out=kt[:, :],
                                           in_=KT.ap()[h * HD:(h + 1) * HD,
                                                       kb * 128:(kb + 1) * 128]), 16)
                        qL.inc(s.dma_start(out=vt[:, :],
                                           in_=VT.ap()[kb * 128:(kb + 1) * 128,
                                                       h * HD:(h + 1) * HD]), 16)
                    @blk.tensor
                    def _(t, pb=pb, kt=kt):
                        t.wait_ge(qL.h, qL.n)
                        t.wait_ge(eV.h, eV.n)
                        t.wait_ge(eS.h, eS.n)
                        eT.inc(t.matmul(pb[:, :], kt[:, :], qtg1[:, :], start=True, stop=True))
                    @blk.scalar
                    def _(s, pb=pb, eb=eb, kb=kb):
                        s.wait_ge(eT.h, eT.n)
                        s.wait_ge(eT.h, eT.n)
                        eS.inc(s.activation(eb[:, :], pb[:, :], AF.Exp,
                                            scale=invrk_cols[:, kb:kb + 1], bias=shiftcol[:, 0:1]))
                    use = meb
                    @blk.vector
                    def _(v, eb=eb, meb=meb, j=j, g_=g_):
                        v.wait_ge(eS.h, eS.n)
                        v.wait_ge(eT.h, eT.n)
                        v.wait_ge(qL.h, qL.n)
                        sl = mask_slots[j]
                        eV.inc(v.tensor_tensor(meb[:, :], eb[:, :],
                                               expt[:, sl * 512:(sl + 1) * 512], ALU.mult))
                    @blk.tensor
                    def _(t, use=use, vt=vt, pq=pq, idx=idx, n=len(vblocks)):
                        t.wait_ge(eV.h, eV.n)
                        t.matmul(pq[:, :], vt[:, :], use[:, :],
                                 start=(idx == 0), stop=(idx == n - 1))
                        mm = t.matmul(z0[0:128, :], onesmat[:, :], use[:, :],
                                      start=(idx == 0), stop=(idx == n - 1))
                        eT.inc(mm)
                    fence()
                @blk.vector
                def _(v, pq=pq, h=h):
                    v.wait_ge(eT.h, eT.n)
                    v.tensor_copy(ot_f32[:, :], pq[:, :])
                    v.reciprocal(invz_bc[:, :], z0[0:128, :])
                    v.drain()
                    eV.inc(v.tensor_tensor(otg[:, h * 512:(h + 1) * 512], ot_f32[:, :],
                                           invz_bc[:, :], ALU.mult))
                fence()
            fence()
            for dn in range(4):
                snapO = {}
                for fc in range(NK):
                    wob = SOB[fc % 2]
                    @blk.gpsimd
                    def _(g, fc=fc, dn=dn, wob=wob):
                        if fc >= 2:
                            g.wait_ge(eT.h, snapO[fc - 2])
                        else:
                            g.wait_ge(eT.h, eT.n)
                        qG.inc(g.dma_start(out=wob[:, :],
                                           in_=wo_in.ap()[fc][:, dn * 512:(dn + 1) * 512]), 16)
                    @blk.tensor
                    def _(t, fc=fc, dn=dn, wob=wob, g_=g_):
                        t.wait_ge(qG.h, qG.n)
                        if fc == 0:
                            t.wait_ge(eV.h, eV.n)
                        for tb in range(4):
                            mm = t.matmul(PPALL[tb][:, :],
                                          otg[:, fc * 512 + tb * 128: fc * 512 + tb * 128 + 128],
                                          wob[:, :], start=(fc == 0), stop=(fc == NK - 1))
                        eT.inc(mm)
                    snapO[fc] = eT.n
                for tb in range(4):
                    m = g_ * 4 + tb
                    @blk.vector
                    def _(v, tb=tb):
                        v.wait_ge(eT.h, eT.n)
                        v.wait_ge(qG.h, qG.n)
                        eV.inc(v.tensor_copy(ot_f32[:, :], PPALL[tb][:, :]))
                    @blk.gpsimd
                    def _(g, m=m, dn=dn):
                        g.wait_ge(eV.h, eV.n)
                        qG.inc(g.dma_start(out=xdst_X3(m, dn * 512, (dn + 1) * 512),
                                           in_=ot_f32[:, :], accum_op=ALU.add), 16)
            fence()
        fence()
        if EMIT_FFN2:
            ffn_call("ffn2", xsrc_X3, xdst_out, stages=FFN_STAGES)
    else:
        @blk.sync
        def _(s):
            for m in range(NT):
                qL.inc(s.dma_start(out=out.ap()[m * 128:(m + 1) * 128, :],
                                   in_=X2.ap()[m * 128:(m + 1) * 128, :]), 16)

    fence()
    @blk.sync
    def _(s):
        for name, (src, dst) in tap_out.items():
            qL.inc(s.dma_start(out=dst.ap(), in_=src.ap()), 16)
    sbuf_tap_map = {"gam_bc": (gam_bc, BF16), "bet_bc": (bet_bc, BF16), "scale_bc": (scale_bc, F32),
                    "sin2": (sin2, BF16), "cos2": (cos2, BF16), "invrq_g": (invrq_g, F32),
                    "zpad": (zpad, BF16), "zpad_inv": (zpad_inv, F32), "invzc_cols": (invzc_cols, F32),
                    "zrow_ih": (zrow_ih, F32), "invrk_cols": (invrk_cols, F32), "invz_bc": (invz_bc, F32), "otg": (xg, BF16), "qtg1": (qtg1, BF16), "ebuf0": (ebuf0, BF16), "ot_f32": (ot_f32, F32), "row_a": (row_a_t, F32), "htile": (htile, BF16), "httile": (httile, BF16), "qtg1b": (qtg1, BF16), "qtg0b": (qtg0, BF16), "st_m": (st_m, F32), "st_s": (st_s, F32), "st_t": (st_t, F32), "st_u": (st_u, F32), "lnt": (lnt, BF16), "xnt": (xnt, BF16), "xt": (xt, F32), "xt2": (xt2, F32)}
    @blk.sync
    def _(s):
        for name in SBUF_TAPS:
            t, dt = sbuf_tap_map[name]
            shp = [t.shape[0], t.shape[1]] if hasattr(t, "shape") else None
            o = nc.dram_tensor("stap_" + name, list(t.shape), dt, kind="ExternalOutput")
            qL.inc(s.dma_start(out=o.ap(), in_=t[:, :]), 16)
    fence()





from concourse.bass_utils import run_bass_kernel_spmd


_NC_CACHE = {}


def kernel(**inputs):
    if "nc" not in _NC_CACHE:
        _NC_CACHE["nc"] = build_program(debug_taps=())
    nc = _NC_CACHE["nc"]
    consts, _, _ = host_consts(N_MASKS)
    w = host_weights(inputs)
    x_all = np.ascontiguousarray(np.asarray(inputs["L0_attn_hidden_states"], dtype=np.float32))
    in_maps = []
    for r in range(8):
        im = {"x": np.ascontiguousarray(x_all[r % 4])}
        im.update(w)
        im.update(consts)
        in_maps.append(im)
    res = run_bass_kernel_spmd(nc, in_maps, core_ids=list(range(8)))
    out = np.stack([np.asarray(res.results[b]["out"], dtype=np.float32) for b in range(4)], axis=0)
    return out
